# version 04
# baseline (speedup 1.0000x reference)
import math
import numpy as np
import concourse.bass as bass
import concourse.mybir as mybir
from concourse.bass_utils import run_bass_kernel_spmd
from concourse.alu_op_type import AluOpType as OP
from contextlib import ExitStack

F32 = mybir.dt.float32
BF16 = mybir.dt.bfloat16
I32 = mybir.dt.int32
AF = mybir.ActivationFunctionType

T = 2048
D = 2048
H = 8
DK = 256
DV = 512
HDV = H * DV
INTER = 5504
NKT = D // 128
MT = INTER // 128
EPS = 1e-5
ROPE_BASE = 10000.0
SCALE = DK ** -0.5
GAMMAS = [1.0 - 2.0 ** (-5.0 - h) for h in range(H)]
PI = math.pi

ENGS = ("SY", "T", "S", "V", "G")
SEMS = {"T": ("T", 1), "S": ("S", 1), "V": ("V", 1), "G": ("G", 1),
        "SYd": ("SY", 16), "Gd": ("G", 16), "Out": ("SY", 16)}


class Slot:
    __slots__ = ("ready", "free")

    def __init__(self):
        self.ready = None
        self.free = None


class B:
    def __init__(self, nc):
        self.nc = nc
        self.streams = {e: [] for e in ENGS}
        self.sems = dict(SEMS)
        self.cnt = {s: 0 for s in SEMS}
        self.semobj = {}
        self._stack = None

    def add_sem(self, name):
        if name not in self.sems:
            self.sems[name] = (None, 16)
            self.cnt[name] = 0
            self.semobj[name] = self._stack.enter_context(
                self.nc.semaphore(f"sem_{name}"))
        return name

    def op(self, st, fn, ev=True):
        inc = None
        if ev:
            self.cnt[st] += 1
            inc = (st, 1)
        self.streams[st].append(("i", fn, inc))
        return (st, self.cnt[st]) if ev else None

    def dma(self, st, out, in_, sem=None):
        sem = sem or ("Gd" if st == "G" else "SYd")
        self.cnt[sem] += 16
        self.streams[st].append(
            ("i", lambda e, _o=out, _i=in_: e.dma_start(out=_o, in_=_i), (sem, 16)))
        return (sem, self.cnt[sem])

    def wait(self, st, ev):
        if ev is None:
            return
        sem, val = ev
        if self.sems[sem][0] == st:
            return
        self.streams[st].append(("w", sem, val))

    def total(self, sem):
        return (sem, self.cnt[sem])

    def wait_force(self, st, ev):
        if ev is None:
            return
        sem, val = ev
        self.streams[st].append(("w", sem, val))

    def aw(self, st, slot):
        self.wait(st, slot.free)

    def ar(self, st, slot):
        self.wait(st, slot.ready)

    def flush(self, name):
        nc = self.nc
        streams = {e: list(self.streams[e]) for e in ENGS}
        for e in ENGS:
            self.streams[e] = []
        if not any(streams.values()):
            return

        def mk(e):
            def run(eng):
                for it in streams[e]:
                    if it[0] == "w":
                        eng.wait_ge(self.semobj[it[1]], it[2])
                    else:
                        ins = it[1](eng)
                        if it[2] is not None:
                            ins.then_inc(self.semobj[it[2][0]], it[2][1])
            return run

        with nc.Block(name) as blk:
            if streams["SY"]:
                blk.sync(mk("SY"))
            if streams["T"]:
                blk.tensor(mk("T"))
            if streams["S"]:
                blk.scalar(mk("S"))
            if streams["V"]:
                blk.vector(mk("V"))
            if streams["G"]:
                blk.gpsimd(mk("G"))


def build_program(dbg=()):
    nc = bass.Bass("TRN2", target_bir_lowering=False)
    b = B(nc)
    dk = lambda n: "ExternalOutput" if n in dbg else "Internal"

    xT = nc.dram_tensor("xT", [D, T], F32, kind="ExternalInput").ap()
    wq = nc.dram_tensor("wq", [D, D], F32, kind="ExternalInput").ap()
    wk = nc.dram_tensor("wk", [D, D], F32, kind="ExternalInput").ap()
    wv = nc.dram_tensor("wv", [D, HDV], F32, kind="ExternalInput").ap()
    wgt = nc.dram_tensor("wgt", [D, HDV], F32, kind="ExternalInput").ap()
    wo = nc.dram_tensor("wo", [HDV, D], F32, kind="ExternalInput").ap()
    gn = nc.dram_tensor("gn", [DV, 1], F32, kind="ExternalInput").ap()
    lut = nc.dram_tensor("lut", [1, 16], F32, kind="ExternalInput").ap()
    gidxT = nc.dram_tensor("gidxT", [D, INTER], I32, kind="ExternalInput").ap()
    uidxT = nc.dram_tensor("uidxT", [D, INTER], I32, kind="ExternalInput").ap()
    didxT = nc.dram_tensor("didxT", [INTER, D], I32, kind="ExternalInput").ap()
    outT = nc.dram_tensor("outT", [D, T], F32, kind="ExternalOutput").ap()

    qT_s = nc.dram_tensor("qT_s", [D, T], BF16, kind=dk("qT_s")).ap()
    kT_s = nc.dram_tensor("kT_s", [D, T], BF16, kind=dk("kT_s")).ap()
    v_s = nc.dram_tensor("v_s", [T, HDV], BF16, kind=dk("v_s")).ap()
    gT_s = nc.dram_tensor("gT_s", [HDV, T], BF16, kind=dk("gT_s")).ap()
    x2_s = nc.dram_tensor("x2_s", [D, T], F32, kind=dk("x2_s")).ap()
    wg_s = nc.dram_tensor("wg_s", [D, INTER], BF16, kind=dk("wg_s")).ap()
    wu_s = nc.dram_tensor("wu_s", [D, INTER], BF16, kind=dk("wu_s")).ap()
    wd_s = nc.dram_tensor("wd_s", [INTER, D], BF16, kind=dk("wd_s")).ap()

    top = ExitStack()
    sbt = lambda n, s, d: top.enter_context(nc.sbuf_tensor(n, s, d))

    with top:
        b._stack = top
        for s in SEMS:
            b.semobj[s] = top.enter_context(nc.semaphore(f"sem_{s}"))

        lutrep = sbt("lutrep", [128, 16], F32)
        gncol = sbt("gncol", [128, 4], F32)
        ones_bf = sbt("ones_bf", [128, 1], BF16)
        onesrow_f = sbt("onesrow_f", [1, 128], F32)
        epscol = sbt("epscol", [128, 1], F32)
        dmat = [sbt(f"dmat{h}", [128, 512], BF16) for h in range(H)]
        masks = [sbt(f"mask{i}", [128, 512], BF16) for i in range(4)]
        gidxb = [sbt(f"gidxb{i}", [128, 1024], BF16) for i in range(2)]
        gacc = [sbt(f"gacc{i}", [128, 1024], BF16) for i in range(2)]
        gtmp = sbt("gtmp", [128, 1024], BF16)

        pb = [top.enter_context(nc.psum_tensor(f"pb{i}", [128, 512], F32))
              for i in range(8)]
        pbs = [Slot() for _ in range(8)]

        gidx_slots = [Slot(), Slot()]
        gacc_slots = [Slot(), Slot()]
        GCH = []
        for mat, src, dst in (("wg", gidxT, wg_s), ("wu", uidxT, wu_s)):
            for k in range(NKT):
                for c0 in range(0, INTER, 1024):
                    GCH.append((mat, src, dst, k, c0, min(1024, INTER - c0)))
        for k2 in range(MT):
            for c0 in (0, 1024):
                GCH.append(("wd", didxT, wd_s, k2, c0, 1024))
        G_DONE = {}
        gq = iter(range(len(GCH)))

        for _s in ("Gi0", "Gi1", "Sg_wg0", "Sg_wg1", "Sg_wu0", "Sg_wu1",
                   "Sg_wd0", "Sg_wd1"):
            b.add_sem(_s)

        def tick(n=1):
            for _ in range(n):
                ci = next(gq, None)
                if ci is None:
                    return
                mat, src, dst, k, c0, w = GCH[ci]
                sl = ci % 2
                b.aw("G", gidx_slots[sl])
                e_ld = b.dma("G", gidxb[sl][:, 0:w],
                             src[k * 128:(k + 1) * 128, c0:c0 + w],
                             sem=f"Gi{sl}")
                gidx_slots[sl].ready = e_ld
                b.ar("V", gidx_slots[sl])
                b.aw("V", gacc_slots[sl])
                e_acc = None
                for vv in range(16):
                    b.op("V", lambda e, _s=sl, _w=w, _v=vv: e.tensor_scalar(
                        gtmp[:, 0:_w], gidxb[_s][:, 0:_w], float(_v),
                        lutrep[:, _v:_v + 1], OP.is_equal, OP.mult), ev=False)
                    if vv == 0:
                        b.op("V", lambda e, _s=sl, _w=w: e.tensor_copy(
                            gacc[_s][:, 0:_w], gtmp[:, 0:_w]), ev=False)
                    else:
                        e_acc = b.op("V", lambda e, _s=sl, _w=w: e.tensor_tensor(
                            gacc[_s][:, 0:_w], gacc[_s][:, 0:_w], gtmp[:, 0:_w],
                            OP.add), ev=(vv == 15))
                gidx_slots[sl].free = e_acc
                gacc_slots[sl].ready = e_acc
                b.ar("SY", gacc_slots[sl])
                e_sp = b.dma("SY", dst[k * 128:(k + 1) * 128, c0:c0 + w],
                             gacc[sl][:, 0:w], sem=f"Sg_{mat}{sl}")
                gacc_slots[sl].free = e_sp
                G_DONE[mat] = (b.total(f"Sg_{mat}0"), b.total(f"Sg_{mat}1"))

        e_const = [None]

        def ln_stats(src_dram, mean_row, var_row, srow, scope, tag):
            s_ = lambda n, s, d: scope.enter_context(nc.sbuf_tensor(n, s, d))
            xf = [s_(f"lsxf{tag}{i}", [128, T], F32) for i in range(2)]
            xbf = [s_(f"lsxb{tag}{i}", [128, T], BF16) for i in range(2)]
            sqbf = [s_(f"lssq{tag}{i}", [128, T], BF16) for i in range(2)]
            b.add_sem(f"Lx{tag}0")
            b.add_sem(f"Lx{tag}1")
            xfs = [Slot(), Slot()]
            xbs = [Slot(), Slot()]
            e_last = None
            for k in range(NKT):
                sl = k % 2
                b.aw("SY", xfs[sl])
                e_x = b.dma("SY", xf[sl][:, :], src_dram[k * 128:(k + 1) * 128, :],
                            sem=f"Lx{tag}{sl}")
                xfs[sl].ready = e_x
                b.ar("S", xfs[sl])
                b.aw("S", xbs[sl])
                b.op("S", lambda e, _s=sl: e.copy(xbf[_s][:, :], xf[_s][:, :]),
                     ev=False)
                e_sq = b.op("S", lambda e, _s=sl: e.activation(
                    sqbf[_s][:, :], xf[_s][:, :], AF.Square))
                xfs[sl].free = e_sq
                xbs[sl].ready = e_sq
                b.ar("T", xbs[sl])
                if k == 0:
                    b.wait("T", e_const[0])
                    for n in range(8):
                        b.aw("T", pbs[n])
                for n in range(4):
                    b.op("T", lambda e, _n=n, _s=sl, _k=k: e.matmul(
                        pb[_n][0:1, :], ones_bf[:, :],
                        xbf[_s][:, _n * 512:(_n + 1) * 512],
                        start=(_k == 0), stop=(_k == NKT - 1)), ev=False)
                for n in range(4):
                    e_mm = b.op("T", lambda e, _n=n, _s=sl, _k=k: e.matmul(
                        pb[4 + _n][0:1, :], ones_bf[:, :],
                        sqbf[_s][:, _n * 512:(_n + 1) * 512],
                        start=(_k == 0), stop=(_k == NKT - 1)), ev=(n == 3))
                xbs[sl].free = e_mm
                e_last = e_mm
                tick(1)
            for n in range(8):
                pbs[n].ready = e_last
            b.wait("S", e_last)
            for n in range(4):
                b.op("S", lambda e, _n=n: e.activation(
                    mean_row[0:1, _n * 512:(_n + 1) * 512], pb[_n][0:1, :],
                    AF.Copy, scale=1.0 / D), ev=False)
            for n in range(4):
                e_ev = b.op("S", lambda e, _n=n: e.activation(
                    var_row[0:1, _n * 512:(_n + 1) * 512], pb[4 + _n][0:1, :],
                    AF.Copy, scale=1.0 / D), ev=(n == 3))
            for n in range(8):
                pbs[n].free = e_ev
            b.wait("V", e_ev)
            b.op("V", lambda e: e.tensor_tensor(srow[0:1, :], mean_row[0:1, :],
                                                mean_row[0:1, :], OP.mult), ev=False)
            e_var = b.op("V", lambda e: e.tensor_tensor(
                var_row[0:1, :], var_row[0:1, :], srow[0:1, :], OP.subtract))
            b.wait("S", e_var)
            e_sd = b.op("S", lambda e: e.activation(
                var_row[0:1, :], var_row[0:1, :], AF.Sqrt, bias=epscol[0:1, 0:1]))
            b.wait("V", e_sd)
            e_istd = b.op("V", lambda e: e.reciprocal(var_row[0:1, :],
                                                      var_row[0:1, :]))
            return e_istd

        def bcast_rows(pairs, e_dep, tag):
            sm = b.add_sem(f"Bc{tag}")
            b.wait("SY", e_dep)
            e_rep = None
            for (row, rep) in pairs:
                if e_rep is not None:
                    b.wait("SY", e_rep)
                e_rep = b.dma("SY", rep[0:1, :], row[0:1, :], sem=sm)
                n = 1
                while n < 128:
                    b.wait("SY", e_rep)
                    e_rep = b.dma("SY", rep[n:min(2 * n, 128), :], rep[0:n, :],
                                  sem=sm)
                    n *= 2
            return e_rep

        sc2 = ExitStack()
        with sc2:
            cos_t = sc2.enter_context(nc.sbuf_tensor("cos_t", [128, T], F32))
            sin_t = sc2.enter_context(nc.sbuf_tensor("sin_t", [128, T], F32))
            xln = sc2.enter_context(nc.sbuf_tensor("xln", [128, NKT * T], BF16))

            p0 = ExitStack()
            s0 = lambda n, s, d: p0.enter_context(nc.sbuf_tensor(n, s, d))
            with p0:
                iot = s0("iot", [128, T], F32)
                us = s0("us", [128, T], F32)
                uc = s0("uc", [128, T], F32)
                ui = s0("ui", [128, T], I32)
                uf = s0("uf", [128, T], F32)
                invf = s0("invf", [128, 1], F32)
                lnb = s0("lnb", [128, 1], F32)
                dio = s0("dio", [128, 512], F32)
                miot = s0("miot", [128, 512], F32)
                lrow = s0("lrow", [1, 16], F32)

                b.add_sem("SYl")
                b.add_sem("SYm")
                e_lut = b.dma("SY", lrow[0:1, :], lut[0:1, :], sem="SYl")
                for c in range(4):
                    e_gnl = b.dma("SY", gncol[:, c:c + 1],
                                  gn[c * 128:(c + 1) * 128, 0:1], sem="SYm")

                b.op("G", lambda e: e.memset(ones_bf[:, :], 1.0), ev=False)
                b.op("G", lambda e: e.memset(onesrow_f[:, :], 1.0), ev=False)
                b.op("G", lambda e: e.memset(epscol[:, :], EPS), ev=False)
                b.op("G", lambda e: e.memset(lnb[:, :], -math.log(2 * PI)), ev=False)
                b.op("G", lambda e: e.iota(iot[:, :], [[1, T]], channel_multiplier=0,
                                           allow_small_or_imprecise_dtypes=True),
                     ev=False)
                e_pc = b.op("G", lambda e: e.iota(
                    invf[:, :], [[0, 1]], channel_multiplier=1,
                    allow_small_or_imprecise_dtypes=True))
                e_dio = b.op("G", lambda e: e.iota(
                    dio[:, :], [[1, 512]], channel_multiplier=-1,
                    allow_small_or_imprecise_dtypes=True))
                e_const[0] = e_dio
                b.add_sem("Glut")
                b.wait("G", e_lut)
                e_lr = b.dma("G", lutrep[0:1, :], lrow[0:1, :], sem="Glut")
                n = 1
                while n < 128:
                    b.wait("G", e_lr)
                    e_lr = b.dma("G", lutrep[n:min(2 * n, 128), :], lutrep[0:n, :],
                                 sem="Glut")
                    n *= 2
                b.wait("V", e_lr)

                b.wait("S", e_pc)
                e_invf = b.op("S", lambda e: e.activation(
                    invf[:, :], invf[:, :], AF.Exp,
                    scale=-math.log(ROPE_BASE) / 128.0, bias=lnb[:, 0:1]))
                b.wait("V", e_invf)
                b.op("V", lambda e: e.tensor_scalar(
                    us[:, :], iot[:, :], invf[:, 0:1], None, OP.mult), ev=False)
                b.op("V", lambda e: e.tensor_scalar(
                    uc[:, :], us[:, :], 0.25, None, OP.add), ev=False)
                b.op("V", lambda e: e.tensor_copy(ui[:, :], us[:, :]), ev=False)
                b.op("V", lambda e: e.tensor_copy(uf[:, :], ui[:, :]), ev=False)
                e_us = b.op("V", lambda e: e.tensor_tensor(
                    us[:, :], us[:, :], uf[:, :], OP.subtract))
                b.op("V", lambda e: e.tensor_copy(ui[:, :], uc[:, :]), ev=False)
                b.op("V", lambda e: e.tensor_copy(uf[:, :], ui[:, :]), ev=False)
                e_uc = b.op("V", lambda e: e.tensor_tensor(
                    uc[:, :], uc[:, :], uf[:, :], OP.subtract))
                b.wait("S", e_us)
                b.op("S", lambda e: e.activation(sin_t[:, :], us[:, :], AF.Sin,
                                                 scale=2 * PI), ev=False)
                b.wait("S", e_uc)
                b.op("S", lambda e: e.activation(cos_t[:, :], uc[:, :], AF.Sin,
                                                 scale=2 * PI), ev=False)
                b.wait("S", e_dio)
                for h in range(H):
                    b.op("S", lambda e, _h=h: e.activation(
                        dmat[_h][:, :], dio[:, :], AF.Exp,
                        scale=math.log(GAMMAS[_h])), ev=False)
                for i in range(4):
                    e_mio = b.op("G", lambda e, _i=i: e.iota(
                        miot[:, :], [[1, 512]], base=-128 * _i,
                        channel_multiplier=-1,
                        allow_small_or_imprecise_dtypes=True))
                    b.wait("V", e_mio)
                    e_msk = b.op("V", lambda e, _i=i: e.tensor_scalar(
                        masks[_i][:, :], miot[:, :], -0.5, None, OP.is_gt))
                    if i < 3:
                        b.wait("G", e_msk)
                b.flush("p0")

            lnA = ExitStack()
            with lnA:
                mrep1 = lnA.enter_context(nc.sbuf_tensor("mrep1", [128, T], F32))
                irep1 = lnA.enter_context(nc.sbuf_tensor("irep1", [128, T], F32))
                p05 = ExitStack()
                with p05:
                    mean1 = p05.enter_context(nc.sbuf_tensor("mean1", [1, T], F32))
                    var1 = p05.enter_context(nc.sbuf_tensor("var1", [1, T], F32))
                    srow1 = p05.enter_context(nc.sbuf_tensor("srow1", [1, T], F32))
                    e_istd = ln_stats(xT, mean1, var1, srow1, p05, 'a')
                    e_ln1 = bcast_rows(((mean1, mrep1), (var1, irep1)), e_istd, 'a')
                    if "ln1" in dbg:
                        mean_d = nc.dram_tensor("mean_d", [1, T], F32,
                                                kind="ExternalOutput").ap()
                        var_d = nc.dram_tensor("var_d", [1, T], F32,
                                               kind="ExternalOutput").ap()
                        mrep_d = nc.dram_tensor("mrep_d", [128, T], F32,
                                                kind="ExternalOutput").ap()
                        irep_d = nc.dram_tensor("irep_d", [128, T], F32,
                                                kind="ExternalOutput").ap()
                        b.wait("SY", e_ln1)
                        b.dma("SY", mean_d[0:1, :], mean1[0:1, :])
                        b.dma("SY", var_d[0:1, :], var1[0:1, :])
                        b.dma("SY", mrep_d[:, :], mrep1[:, :])
                        b.dma("SY", irep_d[:, :], irep1[:, :])
                    b.flush("p05")

                p1 = ExitStack()
                s1 = lambda n, s, d: p1.enter_context(nc.sbuf_tensor(n, s, d))
                with p1:
                    xf1 = [s1(f"xf1_{i}", [128, T], F32) for i in range(2)]
                    b.add_sem("P1x0")
                    b.add_sem("P1x1")
                    x1s = [Slot(), Slot()]
                    for k in range(NKT):
                        sl = k % 2
                        b.aw("SY", x1s[sl])
                        e_x = b.dma("SY", xf1[sl][:, :], xT[k * 128:(k + 1) * 128, :],
                                    sem=f"P1x{sl}")
                        x1s[sl].ready = e_x
                        b.ar("G", x1s[sl])
                        if k == 0:
                            b.wait("G", e_ln1)
                        b.op("G", lambda e, _s=sl: e.tensor_tensor(
                            xf1[_s][:, :], xf1[_s][:, :], mrep1[:, :], OP.subtract),
                            ev=False)
                        e_g = b.op("G", lambda e, _s=sl: e.tensor_tensor(
                            xf1[_s][:, :], xf1[_s][:, :], irep1[:, :], OP.mult))
                        b.wait("S", e_g)
                        e_c = b.op("S", lambda e, _s=sl, _k=k: e.copy(
                            xln[:, _k * T:(_k + 1) * T], xf1[_s][:, :]))
                        x1s[sl].free = e_c
                        tick(1)
                    xln_ev = e_c
                    b.flush("p1")

            p2 = ExitStack()
            s2 = lambda n, s, d: p2.enter_context(nc.sbuf_tensor(n, s, d))
            with p2:
                wbuf = s2("wbuf", [128, NKT * 1024], BF16)
                ea = s2("ea", [128, T], F32)
                ebn = s2("ebn", [128, 512], F32)
                r1n = s2("r1n", [128, 512], F32)
                r2n = s2("r2n", [128, 512], F32)
                rtn = s2("rtn", [128, 512], F32)
                rb1 = s2("rb1", [128, T], BF16)
                rb2 = s2("rb2", [128, T], BF16)
                gsl = s2("gsl", [128, T], BF16)
                vsl = s2("vsl", [128, 1024], BF16)
                wbs = Slot()
                eas = Slot()
                ebs = Slot()
                rns = Slot()
                ros = Slot()
                gss = Slot()
                vss = Slot()
                b.wait("T", xln_ev)

                b.add_sem("Gw")
                b.add_sem("SYq")
                b.add_sem("SYgs")
                b.add_sem("SYvs")

                def load_half(wsrc, half):
                    b.aw("G", wbs)
                    e_w = None
                    for k in range(NKT):
                        e_w = b.dma("G", wbuf[:, k * 1024:(k + 1) * 1024],
                                    wsrc[k * 128:(k + 1) * 128,
                                         half * 1024:(half + 1) * 1024], sem="Gw")
                    wbs.ready = e_w
                    b.ar("T", wbs)

                for wn, wsrc, wdst in (("q", wq, qT_s), ("k", wk, kT_s)):
                    for half in range(2):
                        load_half(wsrc, half)
                        for mrel in range(8):
                            m = half * 8 + mrel
                            grp = (mrel % 2) * 4
                            for n in range(4):
                                b.aw("T", pbs[grp + n])
                            e_mm = None
                            for k in range(NKT):
                                for n in range(4):
                                    e_mm = b.op("T", lambda e, _g=grp, _n=n, _k=k,
                                                _m=mrel: e.matmul(
                                                    pb[_g + _n][:, :],
                                                    wbuf[:, _k * 1024 + _m * 128:
                                                         _k * 1024 + (_m + 1) * 128],
                                                    xln[:, _k * T + _n * 512:
                                                        _k * T + (_n + 1) * 512],
                                                    start=(_k == 0),
                                                    stop=(_k == NKT - 1)),
                                                ev=(k == NKT - 1 and n == 3))
                            for n in range(4):
                                pbs[grp + n].ready = e_mm
                            h = m // 2
                            if m % 2 == 0:
                                b.aw("S", eas)
                                b.wait("S", e_mm)
                                for n in range(4):
                                    e_ev = b.op("S", lambda e, _g=grp, _n=n: e.copy(
                                        ea[:, _n * 512:(_n + 1) * 512],
                                        pb[_g + _n][:, :]), ev=(n == 3))
                                for n in range(4):
                                    pbs[grp + n].free = e_ev
                                eas.ready = e_ev
                            else:
                                b.ar("G", eas)
                                b.aw("G", ros)
                                e_rp = None
                                for n in range(4):
                                    b.ar("S", pbs[grp + n])
                                    b.aw("S", ebs)
                                    e_eb = b.op("S", lambda e, _g=grp, _n=n: e.copy(
                                        ebn[:, :], pb[_g + _n][:, :]))
                                    pbs[grp + n].free = e_eb
                                    ebs.ready = e_eb
                                    b.ar("G", ebs)
                                    b.aw("G", rns)
                                    c0 = n * 512
                                    b.op("G", lambda e, _c=c0: e.tensor_tensor(
                                        r1n[:, :], ea[:, _c:_c + 512],
                                        cos_t[:, _c:_c + 512], OP.mult), ev=False)
                                    b.op("G", lambda e, _c=c0: e.tensor_tensor(
                                        rtn[:, :], ebn[:, :],
                                        sin_t[:, _c:_c + 512], OP.mult), ev=False)
                                    b.op("G", lambda e: e.tensor_tensor(
                                        r1n[:, :], r1n[:, :], rtn[:, :],
                                        OP.subtract), ev=False)
                                    b.op("G", lambda e, _c=c0: e.tensor_tensor(
                                        r2n[:, :], ebn[:, :],
                                        cos_t[:, _c:_c + 512], OP.mult), ev=False)
                                    b.op("G", lambda e, _c=c0: e.tensor_tensor(
                                        rtn[:, :], ea[:, _c:_c + 512],
                                        sin_t[:, _c:_c + 512], OP.mult), ev=False)
                                    e_rp = b.op("G", lambda e: e.tensor_tensor(
                                        r2n[:, :], r2n[:, :], rtn[:, :], OP.add))
                                    ebs.free = e_rp
                                    rns.ready = e_rp
                                    b.ar("S", rns)
                                    b.op("S", lambda e, _c=c0: e.copy(
                                        rb1[:, _c:_c + 512], r1n[:, :]), ev=False)
                                    e_cb = b.op("S", lambda e, _c=c0: e.copy(
                                        rb2[:, _c:_c + 512], r2n[:, :]))
                                    rns.free = e_cb
                                eas.free = e_rp
                                ros.ready = e_cb
                                b.ar("SY", ros)
                                r0 = h * 256
                                b.dma("SY", wdst[r0:r0 + 128, :], rb1[:, :],
                                      sem="SYq")
                                e_sp = b.dma("SY", wdst[r0 + 128:r0 + 256, :],
                                             rb2[:, :], sem="SYq")
                                ros.free = e_sp
                            tick(2)
                        wbs.free = e_mm

                for half in range(4):
                    load_half(wgt, half)
                    for mrel in range(8):
                        m = half * 8 + mrel
                        grp = (mrel % 2) * 4
                        for n in range(4):
                            b.aw("T", pbs[grp + n])
                        e_mm = None
                        for k in range(NKT):
                            for n in range(4):
                                e_mm = b.op("T", lambda e, _g=grp, _n=n, _k=k,
                                            _m=mrel: e.matmul(
                                                pb[_g + _n][:, :],
                                                wbuf[:, _k * 1024 + _m * 128:
                                                     _k * 1024 + (_m + 1) * 128],
                                                xln[:, _k * T + _n * 512:
                                                    _k * T + (_n + 1) * 512],
                                                start=(_k == 0),
                                                stop=(_k == NKT - 1)),
                                            ev=(k == NKT - 1 and n == 3))
                        for n in range(4):
                            pbs[grp + n].ready = e_mm
                        b.aw("S", gss)
                        b.wait("S", e_mm)
                        for n in range(4):
                            e_ev = b.op("S", lambda e, _g=grp, _n=n: e.copy(
                                gsl[:, _n * 512:(_n + 1) * 512],
                                pb[_g + _n][:, :]), ev=(n == 3))
                        for n in range(4):
                            pbs[grp + n].free = e_ev
                        gss.ready = e_ev
                        b.ar("SY", gss)
                        e_sp = b.dma("SY", gT_s[m * 128:(m + 1) * 128, :],
                                     gsl[:, :], sem="SYgs")
                        gss.free = e_sp
                        tick(2)
                    wbs.free = e_mm

                for cgrp in range(4):
                    b.aw("G", wbs)
                    e_w = None
                    for k in range(NKT):
                        e_w = b.dma("G", wbuf[:, k * 1024:(k + 1) * 1024],
                                    wv[k * 128:(k + 1) * 128,
                                       cgrp * 1024:(cgrp + 1) * 1024], sem="Gw")
                    wbs.ready = e_w
                    b.ar("T", wbs)
                    for t in range(NKT):
                        grp = (t % 2) * 2
                        b.aw("T", pbs[grp])
                        b.aw("T", pbs[grp + 1])
                        e_mm = None
                        for k in range(NKT):
                            for c in range(2):
                                e_mm = b.op("T", lambda e, _g=grp, _c=c, _k=k, _t=t:
                                            e.matmul(
                                                pb[_g + _c][:, :],
                                                xln[:, _k * T + _t * 128:
                                                    _k * T + (_t + 1) * 128],
                                                wbuf[:, _k * 1024 + _c * 512:
                                                     _k * 1024 + (_c + 1) * 512],
                                                start=(_k == 0),
                                                stop=(_k == NKT - 1)),
                                            ev=(k == NKT - 1 and c == 1))
                        pbs[grp].ready = e_mm
                        pbs[grp + 1].ready = e_mm
                        b.aw("S", vss)
                        b.wait("S", e_mm)
                        for c in range(2):
                            e_ev = b.op("S", lambda e, _g=grp, _c=c: e.copy(
                                vsl[:, _c * 512:(_c + 1) * 512],
                                pb[_g + _c][:, :]), ev=(c == 1))
                        pbs[grp].free = e_ev
                        pbs[grp + 1].free = e_ev
                        vss.ready = e_ev
                        b.ar("SY", vss)
                        e_sp = b.dma("SY", v_s[t * 128:(t + 1) * 128,
                                               cgrp * 1024:(cgrp + 1) * 1024],
                                     vsl[:, :], sem="SYvs")
                        vss.free = e_sp
                        tick(2)
                    wbs.free = e_mm
                b.flush("p2")

        p2_spills = [b.total("SYq"), b.total("SYgs"), b.total("SYvs")]
        p3 = ExitStack()
        s3 = lambda n, s, d: p3.enter_context(nc.sbuf_tensor(n, s, d))
        with p3:
            khb = [s3(f"khb{i}", [128, 2 * T], BF16) for i in range(2)]
            qhb = [s3(f"qhb{i}", [128, 1024], BF16) for i in range(2)]
            vhb = [s3(f"vhb{i}", [128, 512], BF16) for i in range(6)]
            ghb = [s3(f"ghb{i}", [128, 4 * 512], BF16) for i in range(2)]
            stf = [s3(f"stf{i}", [128, 512], F32) for i in range(2)]
            stb = [s3(f"stb{i}", [128, 512], BF16) for i in range(6)]
            osq = s3("osq", [128, 512], BF16)
            rrow = s3("rrow", [1, 512], F32)
            rmsrep = s3("rmsrep", [128, 512], BF16)
            obf = s3("obf", [128, 512], BF16)
            sgb = s3("sgb", [128, 512], BF16)
            zb1 = s3("zb1", [128, 512], BF16)
            zbuf = s3("zbuf", [128, 32 * 512], BF16)
            wosl = [s3(f"wosl{i}", [128, 1024], BF16) for i in range(2)]
            attf = s3("attf", [128, 512], F32)
            xsl = [s3(f"xsl{i}", [128, 512], F32) for i in range(2)]
            x2f = [s3(f"x2f{i}", [128, 512], F32) for i in range(2)]
            khs = [Slot(), Slot()]
            qhs = [Slot(), Slot()]
            vhs = [Slot() for _ in range(6)]
            ghs = [Slot(), Slot()]
            stfs = [Slot(), Slot()]
            stbs = [Slot() for _ in range(6)]
            osqs = Slot()
            rrs = Slot()
            rmss = Slot()
            obfs = Slot()
            sgs = Slot()
            z1s = Slot()
            zbufs = Slot()
            wosls = [Slot(), Slot()]
            attfs = Slot()
            xsls = [Slot(), Slot()]
            x2fs = [Slot(), Slot()]

            for _s in ("K0", "K1", "Q0", "Q1", "Gg0", "Gg1", "X0", "X1",
                       "XS0", "XS1", "W0", "W1"):
                b.add_sem(_s)
            for _s in (f"Vt{i}" for i in range(6)):
                b.add_sem(_s)
            for _e in p2_spills:
                b.wait("SY", _e)
            b.wait("S", b.total("SYm"))
            for isb in range(4):
                njt = 4 * isb + 4
                for h in range(H):
                    hp = h % 2
                    b.aw("SY", khs[hp])
                    b.dma("SY", khb[hp][:, 0:T], kT_s[h * 256:h * 256 + 128, :],
                          sem=f"K{hp}")
                    e_k = b.dma("SY", khb[hp][:, T:2 * T],
                                kT_s[h * 256 + 128:h * 256 + 256, :], sem=f"K{hp}")
                    khs[hp].ready = e_k
                    b.aw("SY", qhs[hp])
                    b.dma("SY", qhb[hp][:, 0:512],
                          qT_s[h * 256:h * 256 + 128, isb * 512:(isb + 1) * 512],
                          sem=f"Q{hp}")
                    e_q = b.dma("SY", qhb[hp][:, 512:1024],
                                qT_s[h * 256 + 128:h * 256 + 256,
                                     isb * 512:(isb + 1) * 512], sem=f"Q{hp}")
                    qhs[hp].ready = e_q
                    b.aw("SY", ghs[hp])
                    e_g = None
                    for c in range(4):
                        e_g = b.dma("SY", ghb[hp][:, c * 512:(c + 1) * 512],
                                    gT_s[h * 512 + c * 128:h * 512 + (c + 1) * 128,
                                         isb * 512:(isb + 1) * 512], sem=f"Gg{hp}")
                    ghs[hp].ready = e_g

                    b.ar("T", khs[hp])
                    b.ar("T", qhs[hp])
                    for n in range(4):
                        b.aw("T", pbs[2 + n])
                    e_omm = None
                    for jt in range(njt):
                        vsl_ = jt % 6
                        b.aw("SY", vhs[vsl_])
                        e_v = b.dma("SY", vhb[vsl_][:, :],
                                    v_s[jt * 128:(jt + 1) * 128,
                                        h * 512:(h + 1) * 512], sem=f"Vt{vsl_}")
                        vhs[vsl_].ready = e_v
                        sbk = jt % 2
                        b.aw("T", pbs[sbk])
                        e_smm = None
                        for c in range(2):
                            e_smm = b.op("T", lambda e, _b=sbk, _c=c, _j=jt, _p=hp:
                                         e.matmul(pb[_b][:, :],
                                                  khb[_p][:, _c * T + _j * 128:
                                                       _c * T + (_j + 1) * 128],
                                                  qhb[_p][:, _c * 512:(_c + 1) * 512],
                                                  start=(_c == 0), stop=(_c == 1)),
                                         ev=(c == 1))
                        pbs[sbk].ready = e_smm
                        ctile = SCALE * (GAMMAS[h] ** (isb * 512 - jt * 128))
                        esl = jt % 2
                        b.ar("S", pbs[sbk])
                        b.aw("S", stfs[esl])
                        e_sev = b.op("S", lambda e, _b=sbk, _c=ctile, _e=esl:
                                     e.activation(stf[_e][:, :], pb[_b][:, :],
                                                  AF.Copy, scale=_c))
                        pbs[sbk].free = e_sev
                        stfs[esl].ready = e_sev
                        ssl = jt % 6
                        b.ar("V", stfs[esl])
                        b.aw("V", stbs[ssl])
                        doff = jt - 4 * isb
                        if doff >= 0:
                            b.op("V", lambda e, _e=esl, _d=doff: e.tensor_tensor(
                                stf[_e][:, :], stf[_e][:, :], masks[_d][:, :],
                                OP.mult), ev=False)
                        e_dec = b.op("V", lambda e, _e=esl, _s=ssl, _h=h:
                                     e.tensor_tensor(stb[_s][:, :], stf[_e][:, :],
                                                     dmat[_h][:, :], OP.mult))
                        stfs[esl].free = e_dec
                        stbs[ssl].ready = e_dec
                        b.ar("T", stbs[ssl])
                        b.ar("T", vhs[vsl_])
                        for dvc in range(4):
                            e_omm = b.op("T", lambda e, _d=dvc, _j=jt, _s=ssl,
                                         _v=vsl_: e.matmul(
                                             pb[2 + _d][:, :],
                                             vhb[_v][:, _d * 128:(_d + 1) * 128],
                                             stb[_s][:, :],
                                             start=(_j == 0), stop=(_j == njt - 1)),
                                         ev=(dvc == 3))
                        stbs[ssl].free = e_omm
                        vhs[vsl_].free = e_omm
                    khs[hp].free = e_smm
                    qhs[hp].free = e_smm
                    for n in range(4):
                        pbs[2 + n].ready = e_omm
                    b.aw("T", pbs[6])
                    e_st = None
                    for dvc in range(4):
                        b.ar("S", pbs[2 + dvc])
                        b.aw("S", osqs)
                        e_sq = b.op("S", lambda e, _d=dvc: e.activation(
                            osq[:, :], pb[2 + _d][:, :], AF.Square))
                        osqs.ready = e_sq
                        b.ar("T", osqs)
                        e_st = b.op("T", lambda e, _d=dvc: e.matmul(
                            pb[6][0:1, :], ones_bf[:, :], osq[:, :],
                            start=(_d == 0), stop=(_d == 3)))
                        osqs.free = e_st
                    pbs[6].ready = e_st
                    b.ar("S", pbs[6])
                    b.aw("S", rrs)
                    e_r1 = b.op("S", lambda e: e.activation(
                        rrow[0:1, :], pb[6][0:1, :], AF.Sqrt,
                        bias=epscol[0:1, 0:1], scale=1.0 / DV))
                    pbs[6].free = e_r1
                    b.wait("V", e_r1)
                    e_r2 = b.op("V", lambda e: e.reciprocal(rrow[0:1, :],
                                                            rrow[0:1, :]))
                    rrs.ready = e_r2
                    b.aw("T", pbs[7])
                    b.ar("T", rrs)
                    e_bc = b.op("T", lambda e: e.matmul(
                        pb[7][:, :], onesrow_f[0:1, :], rrow[0:1, :],
                        start=True, stop=True))
                    rrs.free = e_bc
                    pbs[7].ready = e_bc
                    b.ar("S", pbs[7])
                    b.aw("S", rmss)
                    e_rr = b.op("S", lambda e: e.copy(rmsrep[:, :], pb[7][:, :]))
                    pbs[7].free = e_rr
                    rmss.ready = e_rr
                    b.ar("S", ghs[hp])
                    e_z = None
                    for dvc in range(4):
                        b.aw("S", obfs)
                        e_ob = b.op("S", lambda e, _d=dvc: e.activation(
                            obf[:, :], pb[2 + _d][:, :], AF.Copy,
                            scale=gncol[:, _d:_d + 1]))
                        pbs[2 + dvc].free = e_ob
                        obfs.ready = e_ob
                        b.aw("S", sgs)
                        e_sg = b.op("S", lambda e, _d=dvc, _p=hp: e.activation(
                            sgb[:, :], ghb[_p][:, _d * 512:(_d + 1) * 512],
                            AF.Silu))
                        sgs.ready = e_sg
                        b.ar("V", obfs)
                        b.ar("V", sgs)
                        b.aw("V", z1s)
                        e_z1 = b.op("V", lambda e: e.tensor_tensor(
                            zb1[:, :], obf[:, :], sgb[:, :], OP.mult))
                        obfs.free = e_z1
                        sgs.free = e_z1
                        z1s.ready = e_z1
                        b.ar("V", rmss)
                        k2 = h * 4 + dvc
                        if k2 == 0:
                            b.aw("V", zbufs)
                        e_z = b.op("V", lambda e, _k=k2: e.tensor_tensor(
                            zbuf[:, _k * 512:(_k + 1) * 512], zb1[:, :],
                            rmsrep[:, :], OP.mult))
                        z1s.free = e_z
                        if dvc == 3:
                            rmss.free = e_z
                        if k2 == 31:
                            zbufs.ready = e_z
                    ghs[hp].free = e_z
                    tick(2)

                b.ar("T", zbufs)
                e_wmm = None
                for oh in range(2):
                    for k2 in range(32):
                        wsl = k2 % 2
                        b.aw("G", wosls[wsl])
                        e_wo = b.dma("G", wosl[wsl][:, :],
                                     wo[k2 * 128:(k2 + 1) * 128,
                                        oh * 1024:(oh + 1) * 1024], sem=f"W{wsl}")
                        wosls[wsl].ready = e_wo
                        b.ar("T", wosls[wsl])
                        if k2 == 0:
                            for mrel in range(8):
                                b.aw("T", pbs[mrel])
                        for mrel in range(8):
                            e_wmm = b.op("T", lambda e, _m=mrel, _k=k2, _w=wsl:
                                         e.matmul(pb[_m][:, :],
                                                  wosl[_w][:, _m * 128:
                                                       (_m + 1) * 128],
                                                  zbuf[:, _k * 512:(_k + 1) * 512],
                                                  start=(_k == 0), stop=(_k == 31)),
                                         ev=(mrel == 7))
                        wosls[wsl].free = e_wmm
                    for mrel in range(8):
                        pbs[mrel].ready = e_wmm
                    for mrel in range(8):
                        outc = oh * 8 + mrel
                        psl = mrel % 2
                        b.ar("S", pbs[mrel])
                        b.aw("S", attfs)
                        e_at = b.op("S", lambda e, _m=mrel: e.copy(
                            attf[:, :], pb[_m][:, :]))
                        pbs[mrel].free = e_at
                        attfs.ready = e_at
                        b.aw("SY", xsls[psl])
                        e_xl = b.dma("SY", xsl[psl][:, :],
                                     xT[outc * 128:(outc + 1) * 128,
                                        isb * 512:(isb + 1) * 512], sem=f"X{psl}")
                        xsls[psl].ready = e_xl
                        b.ar("G", attfs)
                        b.ar("G", xsls[psl])
                        b.aw("G", x2fs[psl])
                        e_x2 = b.op("G", lambda e, _p=psl: e.tensor_tensor(
                            x2f[_p][:, :], attf[:, :], xsl[_p][:, :], OP.add))
                        attfs.free = e_x2
                        xsls[psl].free = e_x2
                        x2fs[psl].ready = e_x2
                        b.ar("SY", x2fs[psl])
                        e_sp = b.dma("SY", x2_s[outc * 128:(outc + 1) * 128,
                                                isb * 512:(isb + 1) * 512],
                                     x2f[psl][:, :], sem=f"XS{psl}")
                        x2fs[psl].free = e_sp
                zbufs.free = e_wmm
                tick(2)
            x2_done = [b.total("XS0"), b.total("XS1")]
            b.flush("p3")

        lnB = ExitStack()
        with lnB:
            mrep2 = lnB.enter_context(nc.sbuf_tensor("mrep2", [128, T], F32))
            irep2 = lnB.enter_context(nc.sbuf_tensor("irep2", [128, T], F32))
            p35 = ExitStack()
            with p35:
                mean2 = p35.enter_context(nc.sbuf_tensor("mean2", [1, T], F32))
                var2 = p35.enter_context(nc.sbuf_tensor("var2", [1, T], F32))
                srow2 = p35.enter_context(nc.sbuf_tensor("srow2", [1, T], F32))
                for _e in x2_done:
                    b.wait("SY", _e)
                e_istd2 = ln_stats(x2_s, mean2, var2, srow2, p35, 'b')
                e_ln2 = bcast_rows(((mean2, mrep2), (var2, irep2)), e_istd2, 'b')
                b.flush("p35")

            FG = []
            m0 = 0
            while m0 < MT:
                gw = min(4, MT - m0)
                FG.append((m0, gw))
                m0 += gw
            p4 = ExitStack()
            s4 = lambda n, s, d: p4.enter_context(nc.sbuf_tensor(n, s, d))
            with p4:
                x2c = s4("x2c", [128, NKT * 1024], BF16)
                x2p = [s4(f"x2p{i}", [128, 1024], F32) for i in range(2)]
                hbuf = s4("hbuf", [128, MT * 1024], BF16)
                hup = s4("hup", [128, 1024], BF16)
                fsl = [s4(f"fsl{i}", [128, 1024], BF16) for i in range(2)]
                yf = [s4(f"yf{i}", [128, 1024], F32) for i in range(2)]
                xr = [s4(f"xr{i}", [128, 1024], F32) for i in range(2)]
                ob = [s4(f"ob{i}", [128, 1024], F32) for i in range(2)]
                for _s in ("P4x0", "P4x1", "F0", "F1", "R0", "R1"):
                    b.add_sem(_s)
                x2ps = [Slot(), Slot()]
                x2cs = Slot()
                hbs = [Slot() for _ in range(MT)]
                hups = Slot()
                fsls = [Slot(), Slot()]
                yfs = [Slot(), Slot()]
                xrs = [Slot(), Slot()]
                obs = [Slot(), Slot()]
                n_out = 0
                for tc in range(2):
                    b.aw("G", x2cs)
                    e_c = None
                    for k in range(NKT):
                        sl = k % 2
                        b.aw("SY", x2ps[sl])
                        e_x = b.dma("SY", x2p[sl][:, :],
                                    x2_s[k * 128:(k + 1) * 128,
                                         tc * 1024:(tc + 1) * 1024],
                                    sem=f"P4x{sl}")
                        x2ps[sl].ready = e_x
                        b.ar("G", x2ps[sl])
                        if k == 0 and tc == 0:
                            b.wait("G", e_ln2)
                        b.op("G", lambda e, _s=sl, _t=tc: e.tensor_tensor(
                            x2p[_s][:, :], x2p[_s][:, :],
                            mrep2[:, _t * 1024:(_t + 1) * 1024], OP.subtract),
                            ev=False)
                        e_g = b.op("G", lambda e, _s=sl, _t=tc: e.tensor_tensor(
                            x2p[_s][:, :], x2p[_s][:, :],
                            irep2[:, _t * 1024:(_t + 1) * 1024], OP.mult))
                        b.wait("S", e_g)
                        e_c = b.op("S", lambda e, _s=sl, _k=k: e.copy(
                            x2c[:, _k * 1024:(_k + 1) * 1024], x2p[_s][:, :]))
                        x2ps[sl].free = e_c
                    x2cs.ready = e_c
                    b.ar("T", x2cs)

                    e_mm = None
                    for mat_i, (mname, msrc) in enumerate((("wg", wg_s),
                                                           ("wu", wu_s))):
                        for gi, (g0, gw) in enumerate(FG):
                            for k in range(NKT):
                                wsl = k % 2
                                b.aw("G", fsls[wsl])
                                if gi == 0 and k == 0 and tc == 0:
                                    for _e in G_DONE[mname]:
                                        b.wait("G", _e)
                                e_w = b.dma("G", fsl[wsl][:, 0:gw * 128],
                                            msrc[k * 128:(k + 1) * 128,
                                                 g0 * 128:(g0 + gw) * 128],
                                            sem=f"F{wsl}")
                                fsls[wsl].ready = e_w
                                b.ar("T", fsls[wsl])
                                if k == 0:
                                    for mrel in range(gw):
                                        b.aw("T", pbs[2 * mrel])
                                        b.aw("T", pbs[2 * mrel + 1])
                                for mrel in range(gw):
                                    for tch in range(2):
                                        e_mm = b.op("T", lambda e, _m=mrel, _t=tch,
                                                    _k=k, _w=wsl: e.matmul(
                                                        pb[2 * _m + _t][:, :],
                                                        fsl[_w][:, _m * 128:
                                                             (_m + 1) * 128],
                                                        x2c[:, _k * 1024 + _t * 512:
                                                            _k * 1024 +
                                                            (_t + 1) * 512],
                                                        start=(_k == 0),
                                                        stop=(_k == NKT - 1)),
                                                    ev=(mrel == gw - 1 and
                                                        tch == 1))
                                fsls[wsl].free = e_mm
                            for mrel in range(gw):
                                pbs[2 * mrel].ready = e_mm
                                pbs[2 * mrel + 1].ready = e_mm
                            for mrel in range(gw):
                                m = g0 + mrel
                                if mat_i == 0:
                                    b.ar("S", pbs[2 * mrel])
                                    b.aw("S", hbs[m])
                                    e_ev = None
                                    for tch in range(2):
                                        e_ev = b.op("S", lambda e, _m=m, _r=mrel,
                                                    _t=tch: e.activation(
                                                        hbuf[:, _m * 1024 +
                                                             _t * 512:
                                                             _m * 1024 +
                                                             (_t + 1) * 512],
                                                        pb[2 * _r + _t][:, :],
                                                        AF.Silu), ev=(tch == 1))
                                    pbs[2 * mrel].free = e_ev
                                    pbs[2 * mrel + 1].free = e_ev
                                    hbs[m].ready = e_ev
                                else:
                                    b.ar("S", pbs[2 * mrel])
                                    b.aw("S", hups)
                                    e_ev = None
                                    for tch in range(2):
                                        e_ev = b.op("S", lambda e, _r=mrel, _t=tch:
                                                    e.copy(
                                                        hup[:, _t * 512:
                                                            (_t + 1) * 512],
                                                        pb[2 * _r + _t][:, :]),
                                                    ev=(tch == 1))
                                    pbs[2 * mrel].free = e_ev
                                    pbs[2 * mrel + 1].free = e_ev
                                    hups.ready = e_ev
                                    b.ar("V", hups)
                                    b.ar("V", hbs[m])
                                    e_hm = b.op("V", lambda e, _m=m: e.tensor_tensor(
                                        hbuf[:, _m * 1024:(_m + 1) * 1024],
                                        hbuf[:, _m * 1024:(_m + 1) * 1024],
                                        hup[:, :], OP.mult))
                                    hups.free = e_hm
                                    hbs[m].ready = e_hm
                            tick(2)
                        if mat_i == 1:
                            x2cs.free = e_mm
                    tick(10 ** 6)
                    for dq in range(4):
                        e_mm = None
                        for k2 in range(MT):
                            wsl = k2 % 2
                            b.aw("G", fsls[wsl])
                            if dq == 0 and k2 == 0 and tc == 0:
                                for _e in G_DONE["wd"]:
                                    b.wait("G", _e)
                            e_w = b.dma("G", fsl[wsl][:, 0:512],
                                        wd_s[k2 * 128:(k2 + 1) * 128,
                                             dq * 512:(dq + 1) * 512],
                                        sem=f"F{wsl}")
                            fsls[wsl].ready = e_w
                            b.ar("T", fsls[wsl])
                            if k2 == 0:
                                for mrel in range(4):
                                    b.aw("T", pbs[2 * mrel])
                                    b.aw("T", pbs[2 * mrel + 1])
                                for m in range(MT):
                                    b.ar("T", hbs[m])
                            for mrel in range(4):
                                for tch in range(2):
                                    e_mm = b.op("T", lambda e, _m=mrel, _t=tch,
                                                _k=k2, _w=wsl: e.matmul(
                                                    pb[2 * _m + _t][:, :],
                                                    fsl[_w][:, _m * 128:
                                                         (_m + 1) * 128],
                                                    hbuf[:, _k * 1024 + _t * 512:
                                                         _k * 1024 +
                                                         (_t + 1) * 512],
                                                    start=(_k == 0),
                                                    stop=(_k == MT - 1)),
                                                ev=(mrel == 3 and tch == 1))
                            fsls[wsl].free = e_mm
                        if dq == 3:
                            for m in range(MT):
                                hbs[m].free = e_mm
                        for mrel in range(4):
                            pbs[2 * mrel].ready = e_mm
                            pbs[2 * mrel + 1].ready = e_mm
                        for mrel in range(4):
                            m2 = dq * 4 + mrel
                            psl = mrel % 2
                            b.ar("S", pbs[2 * mrel])
                            b.aw("S", yfs[psl])
                            e_y = None
                            for tch in range(2):
                                e_y = b.op("S", lambda e, _r=mrel, _t=tch, _p=psl:
                                           e.copy(yf[_p][:, _t * 512:
                                                         (_t + 1) * 512],
                                                  pb[2 * _r + _t][:, :]),
                                           ev=(tch == 1))
                            pbs[2 * mrel].free = e_y
                            pbs[2 * mrel + 1].free = e_y
                            yfs[psl].ready = e_y
                            b.aw("SY", xrs[psl])
                            e_xr = b.dma("SY", xr[psl][:, :],
                                         x2_s[m2 * 128:(m2 + 1) * 128,
                                              tc * 1024:(tc + 1) * 1024],
                                         sem=f"R{psl}")
                            xrs[psl].ready = e_xr
                            b.ar("G", yfs[psl])
                            b.ar("G", xrs[psl])
                            b.aw("G", obs[psl])
                            e_o = b.op("G", lambda e, _p=psl: e.tensor_tensor(
                                ob[_p][:, :], yf[_p][:, :], xr[_p][:, :], OP.add))
                            yfs[psl].free = e_o
                            xrs[psl].free = e_o
                            obs[psl].ready = e_o
                            b.ar("SY", obs[psl])
                            e_out = b.dma("SY", outT[m2 * 128:(m2 + 1) * 128,
                                                     tc * 1024:(tc + 1) * 1024],
                                          ob[psl][:, :], sem="Out")
                            obs[psl].free = e_out
                            n_out += 1
                b.wait("SY", ("Out", n_out * 16))
                b.flush("p4")

    return nc


def kernel(**inputs):
    x = np.asarray(inputs["L0_attn_hidden_states"], dtype=np.float32)
    base = {
        "wq": np.ascontiguousarray(np.asarray(inputs["L0_attn_Wq"], dtype=np.float32)),
        "wk": np.ascontiguousarray(np.asarray(inputs["L0_attn_Wk"], dtype=np.float32)),
        "wv": np.ascontiguousarray(np.asarray(inputs["L0_attn_Wv"], dtype=np.float32)),
        "wgt": np.ascontiguousarray(np.asarray(inputs["L0_attn_Wg"], dtype=np.float32)),
        "wo": np.ascontiguousarray(np.asarray(inputs["L0_attn_Wo"], dtype=np.float32)),
        "gn": np.ascontiguousarray(
            np.asarray(inputs["L0_attn_g_norm_weight"], dtype=np.float32).reshape(DV, 1)),
        "lut": np.ascontiguousarray(
            np.asarray(inputs["L0_ffn_lut"], dtype=np.float32).reshape(1, 16)),
        "gidxT": np.ascontiguousarray(np.asarray(inputs["L0_ffn_gate_idx"]).T.astype(np.int32)),
        "uidxT": np.ascontiguousarray(np.asarray(inputs["L0_ffn_up_idx"]).T.astype(np.int32)),
        "didxT": np.ascontiguousarray(np.asarray(inputs["L0_ffn_down_idx"]).T.astype(np.int32)),
    }
    nc = build_program()
    in_maps = []
    for c in range(8):
        m = dict(base)
        m["xT"] = np.ascontiguousarray(x[c].T)
        in_maps.append(m)
    res = run_bass_kernel_spmd(nc, in_maps, core_ids=list(range(8)))
    out = np.stack([np.asarray(res.results[c]["outT"]).T for c in range(8)])
    return out.astype(np.float32)


# version 05
# speedup vs baseline: 1.1996x; 1.1996x over previous
import math
import numpy as np
import concourse.bass as bass
import concourse.mybir as mybir
from concourse.bass_utils import run_bass_kernel_spmd
from concourse.alu_op_type import AluOpType as OP
from contextlib import ExitStack

F32 = mybir.dt.float32
BF16 = mybir.dt.bfloat16
I32 = mybir.dt.int32
AF = mybir.ActivationFunctionType

T = 2048
D = 2048
H = 8
DK = 256
DV = 512
HDV = H * DV
INTER = 5504
NKT = D // 128
MT = INTER // 128
EPS = 1e-5
ROPE_BASE = 10000.0
SCALE = DK ** -0.5
GAMMAS = [1.0 - 2.0 ** (-5.0 - h) for h in range(H)]
PI = math.pi

ENGS = ("SY", "T", "S", "V", "G")
SEMS = {"T": ("T", 1), "S": ("S", 1), "V": ("V", 1), "G": ("G", 1),
        "SYd": ("SY", 16), "Gd": ("G", 16), "Out": ("SY", 16)}


class Slot:
    __slots__ = ("ready", "free")

    def __init__(self):
        self.ready = None
        self.free = None


class B:
    def __init__(self, nc):
        self.nc = nc
        self.streams = {e: [] for e in ENGS}
        self.sems = dict(SEMS)
        self.cnt = {s: 0 for s in SEMS}
        self.semobj = {}
        self._stack = None

    def add_sem(self, name):
        if name not in self.sems:
            self.sems[name] = (None, 16)
            self.cnt[name] = 0
            self.semobj[name] = self._stack.enter_context(
                self.nc.semaphore(f"sem_{name}"))
        return name

    def op(self, st, fn, ev=True):
        inc = None
        if ev:
            self.cnt[st] += 1
            inc = (st, 1)
        self.streams[st].append(("i", fn, inc))
        return (st, self.cnt[st]) if ev else None

    def dma(self, st, out, in_, sem=None):
        sem = sem or ("Gd" if st == "G" else "SYd")
        self.cnt[sem] += 16
        self.streams[st].append(
            ("i", lambda e, _o=out, _i=in_: e.dma_start(out=_o, in_=_i), (sem, 16)))
        return (sem, self.cnt[sem])

    def wait(self, st, ev):
        if ev is None:
            return
        sem, val = ev
        if self.sems[sem][0] == st:
            return
        self.streams[st].append(("w", sem, val))

    def total(self, sem):
        return (sem, self.cnt[sem])

    def wait_force(self, st, ev):
        if ev is None:
            return
        sem, val = ev
        self.streams[st].append(("w", sem, val))

    def aw(self, st, slot):
        self.wait(st, slot.free)

    def ar(self, st, slot):
        self.wait(st, slot.ready)

    def flush(self, name):
        nc = self.nc
        streams = {e: list(self.streams[e]) for e in ENGS}
        for e in ENGS:
            self.streams[e] = []
        if not any(streams.values()):
            return

        def mk(e):
            def run(eng):
                for it in streams[e]:
                    if it[0] == "w":
                        eng.wait_ge(self.semobj[it[1]], it[2])
                    else:
                        ins = it[1](eng)
                        if it[2] is not None:
                            ins.then_inc(self.semobj[it[2][0]], it[2][1])
            return run

        with nc.Block(name) as blk:
            if streams["SY"]:
                blk.sync(mk("SY"))
            if streams["T"]:
                blk.tensor(mk("T"))
            if streams["S"]:
                blk.scalar(mk("S"))
            if streams["V"]:
                blk.vector(mk("V"))
            if streams["G"]:
                blk.gpsimd(mk("G"))


def build_program(dbg=()):
    nc = bass.Bass("TRN2", target_bir_lowering=False)
    b = B(nc)
    dk = lambda n: "ExternalOutput" if n in dbg else "Internal"

    xT = nc.dram_tensor("xT", [D, T], F32, kind="ExternalInput").ap()
    wq = nc.dram_tensor("wq", [D, D], F32, kind="ExternalInput").ap()
    wk = nc.dram_tensor("wk", [D, D], F32, kind="ExternalInput").ap()
    wv = nc.dram_tensor("wv", [D, HDV], F32, kind="ExternalInput").ap()
    wgt = nc.dram_tensor("wgt", [D, HDV], F32, kind="ExternalInput").ap()
    wo = nc.dram_tensor("wo", [HDV, D], F32, kind="ExternalInput").ap()
    gn = nc.dram_tensor("gn", [DV, 1], F32, kind="ExternalInput").ap()
    lut = nc.dram_tensor("lut", [1, 16], F32, kind="ExternalInput").ap()
    gidxT = nc.dram_tensor("gidxT", [D, INTER], I32, kind="ExternalInput").ap()
    uidxT = nc.dram_tensor("uidxT", [D, INTER], I32, kind="ExternalInput").ap()
    didxT = nc.dram_tensor("didxT", [INTER, D], I32, kind="ExternalInput").ap()
    outT = nc.dram_tensor("outT", [D, T], F32, kind="ExternalOutput").ap()

    qT_s = nc.dram_tensor("qT_s", [D, T], BF16, kind=dk("qT_s")).ap()
    kT_s = nc.dram_tensor("kT_s", [D, T], BF16, kind=dk("kT_s")).ap()
    v_s = nc.dram_tensor("v_s", [T, HDV], BF16, kind=dk("v_s")).ap()
    gT_s = nc.dram_tensor("gT_s", [HDV, T], BF16, kind=dk("gT_s")).ap()
    x2_s = nc.dram_tensor("x2_s", [D, T], F32, kind=dk("x2_s")).ap()
    wg_s = nc.dram_tensor("wg_s", [D, INTER], BF16, kind=dk("wg_s")).ap()
    wu_s = nc.dram_tensor("wu_s", [D, INTER], BF16, kind=dk("wu_s")).ap()
    wd_s = nc.dram_tensor("wd_s", [INTER, D], BF16, kind=dk("wd_s")).ap()

    top = ExitStack()
    sbt = lambda n, s, d: top.enter_context(nc.sbuf_tensor(n, s, d))

    with top:
        b._stack = top
        for s in SEMS:
            b.semobj[s] = top.enter_context(nc.semaphore(f"sem_{s}"))

        lutrep = sbt("lutrep", [128, 16], F32)
        gncol = sbt("gncol", [128, 4], F32)
        ones_bf = sbt("ones_bf", [128, 1], BF16)
        onesrow_f = sbt("onesrow_f", [1, 128], F32)
        epscol = sbt("epscol", [128, 1], F32)
        gidxb = [sbt(f"gidxb{i}", [128, 2048], BF16) for i in range(3)]
        gacc = [sbt(f"gacc{i}", [128, 2048], BF16) for i in range(2)]
        gtmp = sbt("gtmp", [128, 2048], BF16)

        pb = [top.enter_context(nc.psum_tensor(f"pb{i}", [128, 512], F32))
              for i in range(8)]
        pbs = [Slot() for _ in range(8)]

        gidx_slots = [Slot(), Slot(), Slot()]
        gacc_slots = [Slot(), Slot()]
        GCH = []
        for mat, src, dst in (("wg", gidxT, wg_s), ("wu", uidxT, wu_s)):
            for k in range(NKT):
                for c0 in range(0, INTER, 2048):
                    GCH.append((mat, src, dst, k, c0, min(2048, INTER - c0)))
        for k2 in range(MT):
            GCH.append(("wd", didxT, wd_s, k2, 0, 2048))
        G_DONE = {}
        gq = iter(range(len(GCH)))

        for _m in ("wg", "wu", "wd"):
            for _i in range(3):
                b.add_sem(f"Sg_{_m}{_i}")
        for _i in range(3):
            b.add_sem(f"Gi{_i}")

        def tick(n=1):
            for _ in range(n):
                ci = next(gq, None)
                if ci is None:
                    return
                mat, src, dst, k, c0, w = GCH[ci]
                sl = ci % 3
                al = ci % 2
                b.aw("G", gidx_slots[sl])
                e_ld = b.dma("G", gidxb[sl][:, 0:w],
                             src[k * 128:(k + 1) * 128, c0:c0 + w],
                             sem=f"Gi{sl}")
                gidx_slots[sl].ready = e_ld
                b.ar("V", gidx_slots[sl])
                b.aw("V", gacc_slots[al])
                e_acc = None
                for vv in range(16):
                    b.op("V", lambda e, _s=sl, _w=w, _v=vv: e.tensor_scalar(
                        gtmp[:, 0:_w], gidxb[_s][:, 0:_w], float(_v),
                        lutrep[:, _v:_v + 1], OP.is_equal, OP.mult), ev=False)
                    if vv == 0:
                        b.op("V", lambda e, _a=al, _w=w: e.tensor_copy(
                            gacc[_a][:, 0:_w], gtmp[:, 0:_w]), ev=False)
                    else:
                        e_acc = b.op("V", lambda e, _a=al, _w=w: e.tensor_tensor(
                            gacc[_a][:, 0:_w], gacc[_a][:, 0:_w], gtmp[:, 0:_w],
                            OP.add), ev=(vv == 15))
                gidx_slots[sl].free = e_acc
                gacc_slots[al].ready = e_acc
                b.ar("G", gacc_slots[al])
                e_sp = b.dma("G", dst[k * 128:(k + 1) * 128, c0:c0 + w],
                             gacc[al][:, 0:w], sem=f"Sg_{mat}{al}")
                gacc_slots[al].free = e_sp
                G_DONE[mat] = tuple(b.total(f"Sg_{mat}{i}") for i in range(2))

        e_const = [None]

        def ln_stats(src_dram, mean_row, var_row, srow, scope, tag):
            s_ = lambda n, s, d: scope.enter_context(nc.sbuf_tensor(n, s, d))
            xf = [s_(f"lsxf{tag}{i}", [128, T], F32) for i in range(2)]
            xbf = [s_(f"lsxb{tag}{i}", [128, T], BF16) for i in range(2)]
            sqbf = [s_(f"lssq{tag}{i}", [128, T], BF16) for i in range(2)]
            b.add_sem(f"Lx{tag}0")
            b.add_sem(f"Lx{tag}1")
            xfs = [Slot(), Slot()]
            xbs = [Slot(), Slot()]
            e_last = None
            for k in range(NKT):
                sl = k % 2
                b.aw("SY", xfs[sl])
                e_x = b.dma("SY", xf[sl][:, :], src_dram[k * 128:(k + 1) * 128, :],
                            sem=f"Lx{tag}{sl}")
                xfs[sl].ready = e_x
                b.ar("S", xfs[sl])
                b.aw("S", xbs[sl])
                b.op("S", lambda e, _s=sl: e.copy(xbf[_s][:, :], xf[_s][:, :]),
                     ev=False)
                e_sq = b.op("S", lambda e, _s=sl: e.activation(
                    sqbf[_s][:, :], xf[_s][:, :], AF.Square))
                xfs[sl].free = e_sq
                xbs[sl].ready = e_sq
                b.ar("T", xbs[sl])
                if k == 0:
                    b.wait("T", e_const[0])
                    for n in range(8):
                        b.aw("T", pbs[n])
                for n in range(4):
                    b.op("T", lambda e, _n=n, _s=sl, _k=k: e.matmul(
                        pb[_n][0:1, :], ones_bf[:, :],
                        xbf[_s][:, _n * 512:(_n + 1) * 512],
                        start=(_k == 0), stop=(_k == NKT - 1)), ev=False)
                for n in range(4):
                    e_mm = b.op("T", lambda e, _n=n, _s=sl, _k=k: e.matmul(
                        pb[4 + _n][0:1, :], ones_bf[:, :],
                        sqbf[_s][:, _n * 512:(_n + 1) * 512],
                        start=(_k == 0), stop=(_k == NKT - 1)), ev=(n == 3))
                xbs[sl].free = e_mm
                e_last = e_mm
                if k % 4 == 3:
                    tick(1)
            for n in range(8):
                pbs[n].ready = e_last
            b.wait("S", e_last)
            for n in range(4):
                b.op("S", lambda e, _n=n: e.activation(
                    mean_row[0:1, _n * 512:(_n + 1) * 512], pb[_n][0:1, :],
                    AF.Copy, scale=1.0 / D), ev=False)
            for n in range(4):
                e_ev = b.op("S", lambda e, _n=n: e.activation(
                    var_row[0:1, _n * 512:(_n + 1) * 512], pb[4 + _n][0:1, :],
                    AF.Copy, scale=1.0 / D), ev=(n == 3))
            for n in range(8):
                pbs[n].free = e_ev
            b.wait("V", e_ev)
            b.op("V", lambda e: e.tensor_tensor(srow[0:1, :], mean_row[0:1, :],
                                                mean_row[0:1, :], OP.mult), ev=False)
            e_var = b.op("V", lambda e: e.tensor_tensor(
                var_row[0:1, :], var_row[0:1, :], srow[0:1, :], OP.subtract))
            b.wait("S", e_var)
            e_sd = b.op("S", lambda e: e.activation(
                var_row[0:1, :], var_row[0:1, :], AF.Sqrt, bias=epscol[0:1, 0:1]))
            b.wait("V", e_sd)
            e_istd = b.op("V", lambda e: e.reciprocal(var_row[0:1, :],
                                                      var_row[0:1, :]))
            return e_istd

        def bcast_rows(pairs, e_dep, tag):
            sm = b.add_sem(f"Bc{tag}")
            b.wait("SY", e_dep)
            e_rep = None
            for (row, rep) in pairs:
                if e_rep is not None:
                    b.wait("SY", e_rep)
                e_rep = b.dma("SY", rep[0:1, :], row[0:1, :], sem=sm)
                n = 1
                while n < 128:
                    b.wait("SY", e_rep)
                    e_rep = b.dma("SY", rep[n:min(2 * n, 128), :], rep[0:n, :],
                                  sem=sm)
                    n *= 2
            return e_rep

        sc2 = ExitStack()
        with sc2:
            cos_t = sc2.enter_context(nc.sbuf_tensor("cos_t", [128, T], F32))
            sin_t = sc2.enter_context(nc.sbuf_tensor("sin_t", [128, T], F32))
            xln = sc2.enter_context(nc.sbuf_tensor("xln", [128, NKT * T], BF16))

            p0 = ExitStack()
            s0 = lambda n, s, d: p0.enter_context(nc.sbuf_tensor(n, s, d))
            with p0:
                iot = s0("iot", [128, T], F32)
                us = s0("us", [128, T], F32)
                uc = s0("uc", [128, T], F32)
                ui = s0("ui", [128, T], I32)
                uf = s0("uf", [128, T], F32)
                invf = s0("invf", [128, 1], F32)
                lnb = s0("lnb", [128, 1], F32)
                lrow = s0("lrow", [1, 16], F32)

                b.add_sem("SYl")
                b.add_sem("SYm")
                e_lut = b.dma("SY", lrow[0:1, :], lut[0:1, :], sem="SYl")
                for c in range(4):
                    e_gnl = b.dma("SY", gncol[:, c:c + 1],
                                  gn[c * 128:(c + 1) * 128, 0:1], sem="SYm")

                b.op("G", lambda e: e.memset(ones_bf[:, :], 1.0), ev=False)
                b.op("G", lambda e: e.memset(onesrow_f[:, :], 1.0), ev=False)
                b.op("G", lambda e: e.memset(epscol[:, :], EPS), ev=False)
                b.op("G", lambda e: e.memset(lnb[:, :], -math.log(2 * PI)), ev=False)
                b.op("G", lambda e: e.iota(iot[:, :], [[1, T]], channel_multiplier=0,
                                           allow_small_or_imprecise_dtypes=True),
                     ev=False)
                e_pc = b.op("G", lambda e: e.iota(
                    invf[:, :], [[0, 1]], channel_multiplier=1,
                    allow_small_or_imprecise_dtypes=True))
                e_const[0] = e_pc
                b.add_sem("Glut")
                b.wait("G", e_lut)
                e_lr = b.dma("G", lutrep[0:1, :], lrow[0:1, :], sem="Glut")
                n = 1
                while n < 128:
                    b.wait("G", e_lr)
                    e_lr = b.dma("G", lutrep[n:min(2 * n, 128), :], lutrep[0:n, :],
                                 sem="Glut")
                    n *= 2
                b.wait("V", e_lr)

                b.wait("S", e_pc)
                e_invf = b.op("S", lambda e: e.activation(
                    invf[:, :], invf[:, :], AF.Exp,
                    scale=-math.log(ROPE_BASE) / 128.0, bias=lnb[:, 0:1]))
                b.wait("V", e_invf)
                b.op("V", lambda e: e.tensor_scalar(
                    us[:, :], iot[:, :], invf[:, 0:1], None, OP.mult), ev=False)
                b.op("V", lambda e: e.tensor_scalar(
                    uc[:, :], us[:, :], 0.25, None, OP.add), ev=False)
                b.op("V", lambda e: e.tensor_copy(ui[:, :], us[:, :]), ev=False)
                b.op("V", lambda e: e.tensor_copy(uf[:, :], ui[:, :]), ev=False)
                e_us = b.op("V", lambda e: e.tensor_tensor(
                    us[:, :], us[:, :], uf[:, :], OP.subtract))
                b.op("V", lambda e: e.tensor_copy(ui[:, :], uc[:, :]), ev=False)
                b.op("V", lambda e: e.tensor_copy(uf[:, :], ui[:, :]), ev=False)
                e_uc = b.op("V", lambda e: e.tensor_tensor(
                    uc[:, :], uc[:, :], uf[:, :], OP.subtract))
                b.wait("S", e_us)
                b.op("S", lambda e: e.activation(sin_t[:, :], us[:, :], AF.Sin,
                                                 scale=2 * PI), ev=False)
                b.wait("S", e_uc)
                b.op("S", lambda e: e.activation(cos_t[:, :], uc[:, :], AF.Sin,
                                                 scale=2 * PI), ev=False)
                b.flush("p0")

            lnA = ExitStack()
            with lnA:
                mrep1 = lnA.enter_context(nc.sbuf_tensor("mrep1", [128, T], F32))
                irep1 = lnA.enter_context(nc.sbuf_tensor("irep1", [128, T], F32))
                p05 = ExitStack()
                with p05:
                    mean1 = p05.enter_context(nc.sbuf_tensor("mean1", [1, T], F32))
                    var1 = p05.enter_context(nc.sbuf_tensor("var1", [1, T], F32))
                    srow1 = p05.enter_context(nc.sbuf_tensor("srow1", [1, T], F32))
                    e_istd = ln_stats(xT, mean1, var1, srow1, p05, 'a')
                    e_ln1 = bcast_rows(((mean1, mrep1), (var1, irep1)), e_istd, 'a')
                    if "ln1" in dbg:
                        mean_d = nc.dram_tensor("mean_d", [1, T], F32,
                                                kind="ExternalOutput").ap()
                        var_d = nc.dram_tensor("var_d", [1, T], F32,
                                               kind="ExternalOutput").ap()
                        mrep_d = nc.dram_tensor("mrep_d", [128, T], F32,
                                                kind="ExternalOutput").ap()
                        irep_d = nc.dram_tensor("irep_d", [128, T], F32,
                                                kind="ExternalOutput").ap()
                        b.wait("SY", e_ln1)
                        b.dma("SY", mean_d[0:1, :], mean1[0:1, :])
                        b.dma("SY", var_d[0:1, :], var1[0:1, :])
                        b.dma("SY", mrep_d[:, :], mrep1[:, :])
                        b.dma("SY", irep_d[:, :], irep1[:, :])
                    b.flush("p05")

                p1 = ExitStack()
                s1 = lambda n, s, d: p1.enter_context(nc.sbuf_tensor(n, s, d))
                with p1:
                    xf1 = [s1(f"xf1_{i}", [128, T], F32) for i in range(2)]
                    b.add_sem("P1x0")
                    b.add_sem("P1x1")
                    x1s = [Slot(), Slot()]
                    for k in range(NKT):
                        sl = k % 2
                        b.aw("SY", x1s[sl])
                        e_x = b.dma("SY", xf1[sl][:, :], xT[k * 128:(k + 1) * 128, :],
                                    sem=f"P1x{sl}")
                        x1s[sl].ready = e_x
                        b.ar("G", x1s[sl])
                        if k == 0:
                            b.wait("G", e_ln1)
                        b.op("G", lambda e, _s=sl: e.tensor_tensor(
                            xf1[_s][:, :], xf1[_s][:, :], mrep1[:, :], OP.subtract),
                            ev=False)
                        e_g = b.op("G", lambda e, _s=sl: e.tensor_tensor(
                            xf1[_s][:, :], xf1[_s][:, :], irep1[:, :], OP.mult))
                        b.wait("S", e_g)
                        e_c = b.op("S", lambda e, _s=sl, _k=k: e.copy(
                            xln[:, _k * T:(_k + 1) * T], xf1[_s][:, :]))
                        x1s[sl].free = e_c
                        if k % 3 == 2:
                            tick(1)
                    xln_ev = e_c
                    b.flush("p1")

            p2 = ExitStack()
            s2 = lambda n, s, d: p2.enter_context(nc.sbuf_tensor(n, s, d))
            with p2:
                wbuf = [s2(f"wbuf{i}", [128, NKT * 1024], BF16) for i in range(2)]
                stg = [s2(f"stg{i}", [128, 1024], F32) for i in range(2)]
                ea = s2("ea", [128, T], F32)
                ebn = s2("ebn", [128, 512], F32)
                r1n = s2("r1n", [128, 512], F32)
                r2n = s2("r2n", [128, 512], F32)
                rtn = s2("rtn", [128, 512], F32)
                rb2 = s2("rb2", [128, T], BF16)
                gsl = s2("gsl", [128, T], BF16)
                rb1 = gsl
                vsl = s2("vsl", [128, 1024], BF16)
                wbs = [Slot(), Slot()]
                stgs = [Slot(), Slot()]
                fill_idx = [0]
                eas = Slot()
                ebs = Slot()
                rns = Slot()
                ros = Slot()
                gss = Slot()
                vss = Slot()
                b.wait("T", xln_ev)

                b.add_sem("St0")
                b.add_sem("St1")
                b.add_sem("SYq")
                b.add_sem("SYgs")
                b.add_sem("SYvs")

                def load_half(wsrc, half):
                    wi = fill_idx[0] % 2
                    fill_idx[0] += 1
                    b.aw("SY", wbs[wi])
                    e_cast = None
                    for k in range(NKT):
                        ssl = k % 2
                        b.aw("SY", stgs[ssl])
                        e_d = b.dma("SY", stg[ssl][:, :],
                                    wsrc[k * 128:(k + 1) * 128,
                                         half * 1024:(half + 1) * 1024],
                                    sem=f"St{ssl}")
                        stgs[ssl].ready = e_d
                        b.ar("S", stgs[ssl])
                        e_cast = b.op("S", lambda e, _w=wi, _k=k, _s=ssl: e.copy(
                            wbuf[_w][:, _k * 1024:(_k + 1) * 1024], stg[_s][:, :]))
                        stgs[ssl].free = e_cast
                    wbs[wi].ready = e_cast
                    b.ar("T", wbs[wi])
                    return wi

                for wn, wsrc, wdst in (("q", wq, qT_s), ("k", wk, kT_s)):
                    for half in range(2):
                        wi = load_half(wsrc, half)
                        for mrel in range(8):
                            m = half * 8 + mrel
                            grp = (mrel % 2) * 4
                            for n in range(4):
                                b.aw("T", pbs[grp + n])
                            e_mm = None
                            for k in range(NKT):
                                for n in range(4):
                                    e_mm = b.op("T", lambda e, _g=grp, _n=n, _k=k,
                                                _m=mrel, _w=wi: e.matmul(
                                                    pb[_g + _n][:, :],
                                                    wbuf[_w][:, _k * 1024 + _m * 128:
                                                         _k * 1024 + (_m + 1) * 128],
                                                    xln[:, _k * T + _n * 512:
                                                        _k * T + (_n + 1) * 512],
                                                    start=(_k == 0),
                                                    stop=(_k == NKT - 1)),
                                                ev=(k == NKT - 1 and n == 3))
                            for n in range(4):
                                pbs[grp + n].ready = e_mm
                            h = m // 2
                            if m % 2 == 0:
                                b.aw("S", eas)
                                b.wait("S", e_mm)
                                for n in range(4):
                                    e_ev = b.op("S", lambda e, _g=grp, _n=n: e.copy(
                                        ea[:, _n * 512:(_n + 1) * 512],
                                        pb[_g + _n][:, :]), ev=(n == 3))
                                for n in range(4):
                                    pbs[grp + n].free = e_ev
                                eas.ready = e_ev
                            else:
                                b.ar("G", eas)
                                b.aw("G", ros)
                                e_rp = None
                                for n in range(4):
                                    b.ar("S", pbs[grp + n])
                                    b.aw("S", ebs)
                                    e_eb = b.op("S", lambda e, _g=grp, _n=n: e.copy(
                                        ebn[:, :], pb[_g + _n][:, :]))
                                    pbs[grp + n].free = e_eb
                                    ebs.ready = e_eb
                                    b.ar("G", ebs)
                                    b.aw("G", rns)
                                    c0 = n * 512
                                    b.op("G", lambda e, _c=c0: e.tensor_tensor(
                                        r1n[:, :], ea[:, _c:_c + 512],
                                        cos_t[:, _c:_c + 512], OP.mult), ev=False)
                                    b.op("G", lambda e, _c=c0: e.tensor_tensor(
                                        rtn[:, :], ebn[:, :],
                                        sin_t[:, _c:_c + 512], OP.mult), ev=False)
                                    b.op("G", lambda e: e.tensor_tensor(
                                        r1n[:, :], r1n[:, :], rtn[:, :],
                                        OP.subtract), ev=False)
                                    b.op("G", lambda e, _c=c0: e.tensor_tensor(
                                        r2n[:, :], ebn[:, :],
                                        cos_t[:, _c:_c + 512], OP.mult), ev=False)
                                    b.op("G", lambda e, _c=c0: e.tensor_tensor(
                                        rtn[:, :], ea[:, _c:_c + 512],
                                        sin_t[:, _c:_c + 512], OP.mult), ev=False)
                                    e_rp = b.op("G", lambda e: e.tensor_tensor(
                                        r2n[:, :], r2n[:, :], rtn[:, :], OP.add))
                                    ebs.free = e_rp
                                    rns.ready = e_rp
                                    b.ar("S", rns)
                                    b.op("S", lambda e, _c=c0: e.copy(
                                        rb1[:, _c:_c + 512], r1n[:, :]), ev=False)
                                    e_cb = b.op("S", lambda e, _c=c0: e.copy(
                                        rb2[:, _c:_c + 512], r2n[:, :]))
                                    rns.free = e_cb
                                eas.free = e_rp
                                ros.ready = e_cb
                                b.ar("SY", ros)
                                r0 = h * 256
                                b.dma("SY", wdst[r0:r0 + 128, :], rb1[:, :],
                                      sem="SYq")
                                e_sp = b.dma("SY", wdst[r0 + 128:r0 + 256, :],
                                             rb2[:, :], sem="SYq")
                                ros.free = e_sp
                        wbs[wi].free = e_mm

                for half in range(4):
                    wi = load_half(wgt, half)
                    for mrel in range(8):
                        m = half * 8 + mrel
                        grp = (mrel % 2) * 4
                        for n in range(4):
                            b.aw("T", pbs[grp + n])
                        e_mm = None
                        for k in range(NKT):
                            for n in range(4):
                                e_mm = b.op("T", lambda e, _g=grp, _n=n, _k=k,
                                            _m=mrel, _w=wi: e.matmul(
                                                pb[_g + _n][:, :],
                                                wbuf[_w][:, _k * 1024 + _m * 128:
                                                     _k * 1024 + (_m + 1) * 128],
                                                xln[:, _k * T + _n * 512:
                                                    _k * T + (_n + 1) * 512],
                                                start=(_k == 0),
                                                stop=(_k == NKT - 1)),
                                            ev=(k == NKT - 1 and n == 3))
                        for n in range(4):
                            pbs[grp + n].ready = e_mm
                        b.aw("S", gss)
                        b.wait("S", e_mm)
                        for n in range(4):
                            e_ev = b.op("S", lambda e, _g=grp, _n=n: e.copy(
                                gsl[:, _n * 512:(_n + 1) * 512],
                                pb[_g + _n][:, :]), ev=(n == 3))
                        for n in range(4):
                            pbs[grp + n].free = e_ev
                        gss.ready = e_ev
                        b.ar("SY", gss)
                        e_sp = b.dma("SY", gT_s[m * 128:(m + 1) * 128, :],
                                     gsl[:, :], sem="SYgs")
                        gss.free = e_sp
                        tick(1)
                    wbs[wi].free = e_mm

                for cgrp in range(4):
                    wi = load_half(wv, cgrp)
                    for t in range(NKT):
                        grp = (t % 2) * 2
                        b.aw("T", pbs[grp])
                        b.aw("T", pbs[grp + 1])
                        e_mm = None
                        for k in range(NKT):
                            for c in range(2):
                                e_mm = b.op("T", lambda e, _g=grp, _c=c, _k=k, _t=t,
                                            _w=wi: e.matmul(
                                                pb[_g + _c][:, :],
                                                xln[:, _k * T + _t * 128:
                                                    _k * T + (_t + 1) * 128],
                                                wbuf[_w][:, _k * 1024 + _c * 512:
                                                     _k * 1024 + (_c + 1) * 512],
                                                start=(_k == 0),
                                                stop=(_k == NKT - 1)),
                                            ev=(k == NKT - 1 and c == 1))
                        pbs[grp].ready = e_mm
                        pbs[grp + 1].ready = e_mm
                        b.aw("S", vss)
                        b.wait("S", e_mm)
                        for c in range(2):
                            e_ev = b.op("S", lambda e, _g=grp, _c=c: e.copy(
                                vsl[:, _c * 512:(_c + 1) * 512],
                                pb[_g + _c][:, :]), ev=(c == 1))
                        pbs[grp].free = e_ev
                        pbs[grp + 1].free = e_ev
                        vss.ready = e_ev
                        b.ar("SY", vss)
                        e_sp = b.dma("SY", v_s[t * 128:(t + 1) * 128,
                                               cgrp * 1024:(cgrp + 1) * 1024],
                                     vsl[:, :], sem="SYvs")
                        vss.free = e_sp
                        if t % 2 == 1:
                            tick(1)
                    wbs[wi].free = e_mm
                b.flush("p2")

        p2_spills = [b.total("SYq"), b.total("SYgs"), b.total("SYvs")]
        p3 = ExitStack()
        s3 = lambda n, s, d: p3.enter_context(nc.sbuf_tensor(n, s, d))
        with p3:
            dmat = [s3(f"dmat{h}", [128, 512], BF16) for h in range(H)]
            masks = [s3(f"mask{i}", [128, 512], BF16) for i in range(4)]
            dio = s3("dio", [128, 512], F32)
            khb = [s3(f"khb{i}", [128, 2 * T], BF16) for i in range(2)]
            qhb = [s3(f"qhb{i}", [128, 1024], BF16) for i in range(2)]
            vhb = [s3(f"vhb{i}", [128, 512], BF16) for i in range(6)]
            ghb = [s3(f"ghb{i}", [128, 4 * 512], BF16) for i in range(2)]
            stf = [s3(f"stf{i}", [128, 512], F32) for i in range(2)]
            stb = [s3(f"stb{i}", [128, 512], BF16) for i in range(6)]
            osq = s3("osq", [128, 512], BF16)
            rrow = s3("rrow", [1, 512], F32)
            rmsrep = s3("rmsrep", [128, 512], BF16)
            obf = s3("obf", [128, 512], BF16)
            sgb = s3("sgb", [128, 512], BF16)
            zb1 = s3("zb1", [128, 512], BF16)
            zbuf = s3("zbuf", [128, 32 * 512], BF16)
            wosl = [s3(f"wosl{i}", [128, 1024], BF16) for i in range(2)]
            stg3 = [s3(f"stg3_{i}", [128, 1024], F32) for i in range(2)]
            attf = s3("attf", [128, 512], F32)
            xsl = [s3(f"xsl{i}", [128, 512], F32) for i in range(2)]
            x2f = [s3(f"x2f{i}", [128, 512], F32) for i in range(2)]
            khs = [Slot(), Slot()]
            qhs = [Slot(), Slot()]
            vhs = [Slot() for _ in range(6)]
            ghs = [Slot(), Slot()]
            stfs = [Slot(), Slot()]
            stbs = [Slot() for _ in range(6)]
            osqs = Slot()
            rrs = Slot()
            rmss = Slot()
            obfs = Slot()
            sgs = Slot()
            z1s = Slot()
            zbufs = Slot()
            wosls = [Slot(), Slot()]
            stg3s = [Slot(), Slot()]
            attfs = Slot()
            b.add_sem("St2")
            b.add_sem("St3")
            xsls = [Slot(), Slot()]
            x2fs = [Slot(), Slot()]

            for _s in ("K0", "K1", "Q0", "Q1", "Gg0", "Gg1", "X0", "X1",
                       "XS0", "XS1", "W0", "W1"):
                b.add_sem(_s)
            for _s in (f"Vt{i}" for i in range(6)):
                b.add_sem(_s)
            for _e in p2_spills:
                b.wait("SY", _e)
            b.wait("S", b.total("SYm"))
            e_dio = b.op("G", lambda e: e.iota(
                dio[:, :], [[1, 512]], channel_multiplier=-1,
                allow_small_or_imprecise_dtypes=True))
            b.wait("S", e_dio)
            e_dm = None
            for h in range(H):
                e_dm = b.op("S", lambda e, _h=h: e.activation(
                    dmat[_h][:, :], dio[:, :], AF.Exp,
                    scale=math.log(GAMMAS[_h])))
            for i in range(4):
                e_mio = b.op("G", lambda e, _i=i: e.iota(
                    dio[:, :], [[1, 512]], base=-128 * _i, channel_multiplier=-1,
                    allow_small_or_imprecise_dtypes=True) if _i > 0 else e.nop())
                if i == 0:
                    b.wait("G", e_dm)
                b.wait("V", e_mio)
                e_msk = b.op("V", lambda e, _i=i: e.tensor_scalar(
                    masks[_i][:, :], dio[:, :], -0.5, None, OP.is_gt))
                if i < 3:
                    b.wait("G", e_msk)
            for isb in range(4):
                njt = 4 * isb + 4
                for h in range(H):
                    hp = h % 2
                    b.aw("SY", khs[hp])
                    b.dma("SY", khb[hp][:, 0:T], kT_s[h * 256:h * 256 + 128, :],
                          sem=f"K{hp}")
                    e_k = b.dma("SY", khb[hp][:, T:2 * T],
                                kT_s[h * 256 + 128:h * 256 + 256, :], sem=f"K{hp}")
                    khs[hp].ready = e_k
                    b.aw("SY", qhs[hp])
                    b.dma("SY", qhb[hp][:, 0:512],
                          qT_s[h * 256:h * 256 + 128, isb * 512:(isb + 1) * 512],
                          sem=f"Q{hp}")
                    e_q = b.dma("SY", qhb[hp][:, 512:1024],
                                qT_s[h * 256 + 128:h * 256 + 256,
                                     isb * 512:(isb + 1) * 512], sem=f"Q{hp}")
                    qhs[hp].ready = e_q
                    b.aw("SY", ghs[hp])
                    e_g = None
                    for c in range(4):
                        e_g = b.dma("SY", ghb[hp][:, c * 512:(c + 1) * 512],
                                    gT_s[h * 512 + c * 128:h * 512 + (c + 1) * 128,
                                         isb * 512:(isb + 1) * 512], sem=f"Gg{hp}")
                    ghs[hp].ready = e_g

                    b.ar("T", khs[hp])
                    b.ar("T", qhs[hp])
                    for n in range(4):
                        b.aw("T", pbs[2 + n])
                    e_omm = None
                    for jt in range(njt):
                        vsl_ = jt % 6
                        b.aw("SY", vhs[vsl_])
                        e_v = b.dma("SY", vhb[vsl_][:, :],
                                    v_s[jt * 128:(jt + 1) * 128,
                                        h * 512:(h + 1) * 512], sem=f"Vt{vsl_}")
                        vhs[vsl_].ready = e_v
                        sbk = jt % 2
                        b.aw("T", pbs[sbk])
                        e_smm = None
                        for c in range(2):
                            e_smm = b.op("T", lambda e, _b=sbk, _c=c, _j=jt, _p=hp:
                                         e.matmul(pb[_b][:, :],
                                                  khb[_p][:, _c * T + _j * 128:
                                                       _c * T + (_j + 1) * 128],
                                                  qhb[_p][:, _c * 512:(_c + 1) * 512],
                                                  start=(_c == 0), stop=(_c == 1)),
                                         ev=(c == 1))
                        pbs[sbk].ready = e_smm
                        ctile = SCALE * (GAMMAS[h] ** (isb * 512 - jt * 128))
                        esl = jt % 2
                        b.ar("S", pbs[sbk])
                        b.aw("S", stfs[esl])
                        e_sev = b.op("S", lambda e, _b=sbk, _c=ctile, _e=esl:
                                     e.activation(stf[_e][:, :], pb[_b][:, :],
                                                  AF.Copy, scale=_c))
                        pbs[sbk].free = e_sev
                        stfs[esl].ready = e_sev
                        ssl = jt % 6
                        b.ar("V", stfs[esl])
                        b.aw("V", stbs[ssl])
                        doff = jt - 4 * isb
                        if doff >= 0:
                            b.op("V", lambda e, _e=esl, _d=doff: e.tensor_tensor(
                                stf[_e][:, :], stf[_e][:, :], masks[_d][:, :],
                                OP.mult), ev=False)
                        e_dec = b.op("V", lambda e, _e=esl, _s=ssl, _h=h:
                                     e.tensor_tensor(stb[_s][:, :], stf[_e][:, :],
                                                     dmat[_h][:, :], OP.mult))
                        stfs[esl].free = e_dec
                        stbs[ssl].ready = e_dec
                        b.ar("T", stbs[ssl])
                        b.ar("T", vhs[vsl_])
                        for dvc in range(4):
                            e_omm = b.op("T", lambda e, _d=dvc, _j=jt, _s=ssl,
                                         _v=vsl_: e.matmul(
                                             pb[2 + _d][:, :],
                                             vhb[_v][:, _d * 128:(_d + 1) * 128],
                                             stb[_s][:, :],
                                             start=(_j == 0), stop=(_j == njt - 1)),
                                         ev=(dvc == 3))
                        stbs[ssl].free = e_omm
                        vhs[vsl_].free = e_omm
                    khs[hp].free = e_smm
                    qhs[hp].free = e_smm
                    for n in range(4):
                        pbs[2 + n].ready = e_omm
                    b.aw("T", pbs[6])
                    e_st = None
                    for dvc in range(4):
                        b.ar("S", pbs[2 + dvc])
                        b.aw("S", osqs)
                        e_sq = b.op("S", lambda e, _d=dvc: e.activation(
                            osq[:, :], pb[2 + _d][:, :], AF.Square))
                        osqs.ready = e_sq
                        b.ar("T", osqs)
                        e_st = b.op("T", lambda e, _d=dvc: e.matmul(
                            pb[6][0:1, :], ones_bf[:, :], osq[:, :],
                            start=(_d == 0), stop=(_d == 3)))
                        osqs.free = e_st
                    pbs[6].ready = e_st
                    b.ar("S", pbs[6])
                    b.aw("S", rrs)
                    e_r1 = b.op("S", lambda e: e.activation(
                        rrow[0:1, :], pb[6][0:1, :], AF.Sqrt,
                        bias=epscol[0:1, 0:1], scale=1.0 / DV))
                    pbs[6].free = e_r1
                    b.wait("V", e_r1)
                    e_r2 = b.op("V", lambda e: e.reciprocal(rrow[0:1, :],
                                                            rrow[0:1, :]))
                    rrs.ready = e_r2
                    b.aw("T", pbs[7])
                    b.ar("T", rrs)
                    e_bc = b.op("T", lambda e: e.matmul(
                        pb[7][:, :], onesrow_f[0:1, :], rrow[0:1, :],
                        start=True, stop=True))
                    rrs.free = e_bc
                    pbs[7].ready = e_bc
                    b.ar("S", pbs[7])
                    b.aw("S", rmss)
                    e_rr = b.op("S", lambda e: e.copy(rmsrep[:, :], pb[7][:, :]))
                    pbs[7].free = e_rr
                    rmss.ready = e_rr
                    b.ar("S", ghs[hp])
                    e_z = None
                    for dvc in range(4):
                        b.aw("S", obfs)
                        e_ob = b.op("S", lambda e, _d=dvc: e.activation(
                            obf[:, :], pb[2 + _d][:, :], AF.Copy,
                            scale=gncol[:, _d:_d + 1]))
                        pbs[2 + dvc].free = e_ob
                        obfs.ready = e_ob
                        b.aw("S", sgs)
                        e_sg = b.op("S", lambda e, _d=dvc, _p=hp: e.activation(
                            sgb[:, :], ghb[_p][:, _d * 512:(_d + 1) * 512],
                            AF.Silu))
                        sgs.ready = e_sg
                        b.ar("V", obfs)
                        b.ar("V", sgs)
                        b.aw("V", z1s)
                        e_z1 = b.op("V", lambda e: e.tensor_tensor(
                            zb1[:, :], obf[:, :], sgb[:, :], OP.mult))
                        obfs.free = e_z1
                        sgs.free = e_z1
                        z1s.ready = e_z1
                        b.ar("V", rmss)
                        k2 = h * 4 + dvc
                        if k2 == 0:
                            b.aw("V", zbufs)
                        e_z = b.op("V", lambda e, _k=k2: e.tensor_tensor(
                            zbuf[:, _k * 512:(_k + 1) * 512], zb1[:, :],
                            rmsrep[:, :], OP.mult))
                        z1s.free = e_z
                        if dvc == 3:
                            rmss.free = e_z
                        if k2 == 31:
                            zbufs.ready = e_z
                    ghs[hp].free = e_z
                    tick(1)

                b.ar("T", zbufs)
                e_wmm = None
                for oh in range(2):
                    for k2 in range(32):
                        wsl = k2 % 2
                        b.aw("SY", stg3s[wsl])
                        e_wo = b.dma("SY", stg3[wsl][:, :],
                                     wo[k2 * 128:(k2 + 1) * 128,
                                        oh * 1024:(oh + 1) * 1024],
                                     sem=f"St{2 + wsl}")
                        stg3s[wsl].ready = e_wo
                        b.ar("S", stg3s[wsl])
                        b.aw("S", wosls[wsl])
                        e_wc = b.op("S", lambda e, _w=wsl: e.copy(
                            wosl[_w][:, :], stg3[_w][:, :]))
                        stg3s[wsl].free = e_wc
                        wosls[wsl].ready = e_wc
                        b.ar("T", wosls[wsl])
                        if k2 == 0:
                            for mrel in range(8):
                                b.aw("T", pbs[mrel])
                        for mrel in range(8):
                            e_wmm = b.op("T", lambda e, _m=mrel, _k=k2, _w=wsl:
                                         e.matmul(pb[_m][:, :],
                                                  wosl[_w][:, _m * 128:
                                                       (_m + 1) * 128],
                                                  zbuf[:, _k * 512:(_k + 1) * 512],
                                                  start=(_k == 0), stop=(_k == 31)),
                                         ev=(mrel == 7))
                        wosls[wsl].free = e_wmm
                    for mrel in range(8):
                        pbs[mrel].ready = e_wmm
                    for mrel in range(8):
                        outc = oh * 8 + mrel
                        psl = mrel % 2
                        b.ar("S", pbs[mrel])
                        b.aw("S", attfs)
                        e_at = b.op("S", lambda e, _m=mrel: e.copy(
                            attf[:, :], pb[_m][:, :]))
                        pbs[mrel].free = e_at
                        attfs.ready = e_at
                        b.aw("SY", xsls[psl])
                        e_xl = b.dma("SY", xsl[psl][:, :],
                                     xT[outc * 128:(outc + 1) * 128,
                                        isb * 512:(isb + 1) * 512], sem=f"X{psl}")
                        xsls[psl].ready = e_xl
                        b.ar("G", attfs)
                        b.ar("G", xsls[psl])
                        b.aw("G", x2fs[psl])
                        e_x2 = b.op("G", lambda e, _p=psl: e.tensor_tensor(
                            x2f[_p][:, :], attf[:, :], xsl[_p][:, :], OP.add))
                        attfs.free = e_x2
                        xsls[psl].free = e_x2
                        x2fs[psl].ready = e_x2
                        b.ar("SY", x2fs[psl])
                        e_sp = b.dma("SY", x2_s[outc * 128:(outc + 1) * 128,
                                                isb * 512:(isb + 1) * 512],
                                     x2f[psl][:, :], sem=f"XS{psl}")
                        x2fs[psl].free = e_sp
                zbufs.free = e_wmm
                tick(2)
            x2_done = [b.total("XS0"), b.total("XS1")]
            b.flush("p3")

        lnB = ExitStack()
        with lnB:
            mrep2 = lnB.enter_context(nc.sbuf_tensor("mrep2", [128, T], F32))
            irep2 = lnB.enter_context(nc.sbuf_tensor("irep2", [128, T], F32))
            p35 = ExitStack()
            with p35:
                mean2 = p35.enter_context(nc.sbuf_tensor("mean2", [1, T], F32))
                var2 = p35.enter_context(nc.sbuf_tensor("var2", [1, T], F32))
                srow2 = p35.enter_context(nc.sbuf_tensor("srow2", [1, T], F32))
                for _e in x2_done:
                    b.wait("SY", _e)
                e_istd2 = ln_stats(x2_s, mean2, var2, srow2, p35, 'b')
                e_ln2 = bcast_rows(((mean2, mrep2), (var2, irep2)), e_istd2, 'b')
                b.flush("p35")

            FG = []
            m0 = 0
            while m0 < MT:
                gw = min(4, MT - m0)
                FG.append((m0, gw))
                m0 += gw
            p4 = ExitStack()
            s4 = lambda n, s, d: p4.enter_context(nc.sbuf_tensor(n, s, d))
            with p4:
                x2c = s4("x2c", [128, NKT * 1024], BF16)
                x2p = [s4(f"x2p{i}", [128, 1024], F32) for i in range(2)]
                hbuf = s4("hbuf", [128, MT * 1024], BF16)
                hup = s4("hup", [128, 1024], BF16)
                fsl = [s4(f"fsl{i}", [128, 1024], BF16) for i in range(2)]
                yf = [s4(f"yf{i}", [128, 1024], F32) for i in range(2)]
                xr = [s4(f"xr{i}", [128, 1024], F32) for i in range(2)]
                ob = [s4(f"ob{i}", [128, 1024], F32) for i in range(2)]
                for _s in ("P4x0", "P4x1", "F0", "F1", "R0", "R1"):
                    b.add_sem(_s)
                x2ps = [Slot(), Slot()]
                x2cs = Slot()
                hbs = [Slot() for _ in range(MT)]
                hups = Slot()
                fsls = [Slot(), Slot()]
                yfs = [Slot(), Slot()]
                xrs = [Slot(), Slot()]
                obs = [Slot(), Slot()]
                n_out = 0
                for tc in range(2):
                    b.aw("G", x2cs)
                    e_c = None
                    for k in range(NKT):
                        sl = k % 2
                        b.aw("SY", x2ps[sl])
                        e_x = b.dma("SY", x2p[sl][:, :],
                                    x2_s[k * 128:(k + 1) * 128,
                                         tc * 1024:(tc + 1) * 1024],
                                    sem=f"P4x{sl}")
                        x2ps[sl].ready = e_x
                        b.ar("G", x2ps[sl])
                        if k == 0 and tc == 0:
                            b.wait("G", e_ln2)
                        b.op("G", lambda e, _s=sl, _t=tc: e.tensor_tensor(
                            x2p[_s][:, :], x2p[_s][:, :],
                            mrep2[:, _t * 1024:(_t + 1) * 1024], OP.subtract),
                            ev=False)
                        e_g = b.op("G", lambda e, _s=sl, _t=tc: e.tensor_tensor(
                            x2p[_s][:, :], x2p[_s][:, :],
                            irep2[:, _t * 1024:(_t + 1) * 1024], OP.mult))
                        b.wait("S", e_g)
                        e_c = b.op("S", lambda e, _s=sl, _k=k: e.copy(
                            x2c[:, _k * 1024:(_k + 1) * 1024], x2p[_s][:, :]))
                        x2ps[sl].free = e_c
                    x2cs.ready = e_c
                    b.ar("T", x2cs)

                    e_mm = None
                    for mat_i, (mname, msrc) in enumerate((("wg", wg_s),
                                                           ("wu", wu_s))):
                        for gi, (g0, gw) in enumerate(FG):
                            for k in range(NKT):
                                wsl = k % 2
                                b.aw("SY", fsls[wsl])
                                if gi == 0 and k == 0 and tc == 0:
                                    for _e in G_DONE[mname]:
                                        b.wait("SY", _e)
                                e_w = b.dma("SY", fsl[wsl][:, 0:gw * 128],
                                            msrc[k * 128:(k + 1) * 128,
                                                 g0 * 128:(g0 + gw) * 128],
                                            sem=f"F{wsl}")
                                fsls[wsl].ready = e_w
                                b.ar("T", fsls[wsl])
                                if k == 0:
                                    for mrel in range(gw):
                                        b.aw("T", pbs[2 * mrel])
                                        b.aw("T", pbs[2 * mrel + 1])
                                for mrel in range(gw):
                                    for tch in range(2):
                                        e_mm = b.op("T", lambda e, _m=mrel, _t=tch,
                                                    _k=k, _w=wsl: e.matmul(
                                                        pb[2 * _m + _t][:, :],
                                                        fsl[_w][:, _m * 128:
                                                             (_m + 1) * 128],
                                                        x2c[:, _k * 1024 + _t * 512:
                                                            _k * 1024 +
                                                            (_t + 1) * 512],
                                                        start=(_k == 0),
                                                        stop=(_k == NKT - 1)),
                                                    ev=(mrel == gw - 1 and
                                                        tch == 1))
                                fsls[wsl].free = e_mm
                            for mrel in range(gw):
                                pbs[2 * mrel].ready = e_mm
                                pbs[2 * mrel + 1].ready = e_mm
                            for mrel in range(gw):
                                m = g0 + mrel
                                if mat_i == 0:
                                    b.ar("S", pbs[2 * mrel])
                                    b.aw("S", hbs[m])
                                    e_ev = None
                                    for tch in range(2):
                                        e_ev = b.op("S", lambda e, _m=m, _r=mrel,
                                                    _t=tch: e.activation(
                                                        hbuf[:, _m * 1024 +
                                                             _t * 512:
                                                             _m * 1024 +
                                                             (_t + 1) * 512],
                                                        pb[2 * _r + _t][:, :],
                                                        AF.Silu), ev=(tch == 1))
                                    pbs[2 * mrel].free = e_ev
                                    pbs[2 * mrel + 1].free = e_ev
                                    hbs[m].ready = e_ev
                                else:
                                    b.ar("S", pbs[2 * mrel])
                                    b.aw("S", hups)
                                    e_ev = None
                                    for tch in range(2):
                                        e_ev = b.op("S", lambda e, _r=mrel, _t=tch:
                                                    e.copy(
                                                        hup[:, _t * 512:
                                                            (_t + 1) * 512],
                                                        pb[2 * _r + _t][:, :]),
                                                    ev=(tch == 1))
                                    pbs[2 * mrel].free = e_ev
                                    pbs[2 * mrel + 1].free = e_ev
                                    hups.ready = e_ev
                                    b.ar("V", hups)
                                    b.ar("V", hbs[m])
                                    e_hm = b.op("V", lambda e, _m=m: e.tensor_tensor(
                                        hbuf[:, _m * 1024:(_m + 1) * 1024],
                                        hbuf[:, _m * 1024:(_m + 1) * 1024],
                                        hup[:, :], OP.mult))
                                    hups.free = e_hm
                                    hbs[m].ready = e_hm
                            tick(2)
                        if mat_i == 1:
                            x2cs.free = e_mm
                    tick(10 ** 6)
                    for dq in range(4):
                        e_mm = None
                        for k2 in range(MT):
                            wsl = k2 % 2
                            b.aw("SY", fsls[wsl])
                            if dq == 0 and k2 == 0 and tc == 0:
                                for _e in G_DONE["wd"]:
                                    b.wait("SY", _e)
                            e_w = b.dma("SY", fsl[wsl][:, 0:512],
                                        wd_s[k2 * 128:(k2 + 1) * 128,
                                             dq * 512:(dq + 1) * 512],
                                        sem=f"F{wsl}")
                            fsls[wsl].ready = e_w
                            b.ar("T", fsls[wsl])
                            if k2 == 0:
                                for mrel in range(4):
                                    b.aw("T", pbs[2 * mrel])
                                    b.aw("T", pbs[2 * mrel + 1])
                                for m in range(MT):
                                    b.ar("T", hbs[m])
                            for mrel in range(4):
                                for tch in range(2):
                                    e_mm = b.op("T", lambda e, _m=mrel, _t=tch,
                                                _k=k2, _w=wsl: e.matmul(
                                                    pb[2 * _m + _t][:, :],
                                                    fsl[_w][:, _m * 128:
                                                         (_m + 1) * 128],
                                                    hbuf[:, _k * 1024 + _t * 512:
                                                         _k * 1024 +
                                                         (_t + 1) * 512],
                                                    start=(_k == 0),
                                                    stop=(_k == MT - 1)),
                                                ev=(mrel == 3 and tch == 1))
                            fsls[wsl].free = e_mm
                        if dq == 3:
                            for m in range(MT):
                                hbs[m].free = e_mm
                        for mrel in range(4):
                            pbs[2 * mrel].ready = e_mm
                            pbs[2 * mrel + 1].ready = e_mm
                        for mrel in range(4):
                            m2 = dq * 4 + mrel
                            psl = mrel % 2
                            b.ar("S", pbs[2 * mrel])
                            b.aw("S", yfs[psl])
                            e_y = None
                            for tch in range(2):
                                e_y = b.op("S", lambda e, _r=mrel, _t=tch, _p=psl:
                                           e.copy(yf[_p][:, _t * 512:
                                                         (_t + 1) * 512],
                                                  pb[2 * _r + _t][:, :]),
                                           ev=(tch == 1))
                            pbs[2 * mrel].free = e_y
                            pbs[2 * mrel + 1].free = e_y
                            yfs[psl].ready = e_y
                            b.aw("SY", xrs[psl])
                            e_xr = b.dma("SY", xr[psl][:, :],
                                         x2_s[m2 * 128:(m2 + 1) * 128,
                                              tc * 1024:(tc + 1) * 1024],
                                         sem=f"R{psl}")
                            xrs[psl].ready = e_xr
                            b.ar("G", yfs[psl])
                            b.ar("G", xrs[psl])
                            b.aw("G", obs[psl])
                            e_o = b.op("G", lambda e, _p=psl: e.tensor_tensor(
                                ob[_p][:, :], yf[_p][:, :], xr[_p][:, :], OP.add))
                            yfs[psl].free = e_o
                            xrs[psl].free = e_o
                            obs[psl].ready = e_o
                            b.ar("SY", obs[psl])
                            e_out = b.dma("SY", outT[m2 * 128:(m2 + 1) * 128,
                                                     tc * 1024:(tc + 1) * 1024],
                                          ob[psl][:, :], sem="Out")
                            obs[psl].free = e_out
                            n_out += 1
                b.wait("SY", ("Out", n_out * 16))
                b.flush("p4")

    return nc


def kernel(**inputs):
    x = np.asarray(inputs["L0_attn_hidden_states"], dtype=np.float32)
    base = {
        "wq": np.ascontiguousarray(np.asarray(inputs["L0_attn_Wq"], dtype=np.float32)),
        "wk": np.ascontiguousarray(np.asarray(inputs["L0_attn_Wk"], dtype=np.float32)),
        "wv": np.ascontiguousarray(np.asarray(inputs["L0_attn_Wv"], dtype=np.float32)),
        "wgt": np.ascontiguousarray(np.asarray(inputs["L0_attn_Wg"], dtype=np.float32)),
        "wo": np.ascontiguousarray(np.asarray(inputs["L0_attn_Wo"], dtype=np.float32)),
        "gn": np.ascontiguousarray(
            np.asarray(inputs["L0_attn_g_norm_weight"], dtype=np.float32).reshape(DV, 1)),
        "lut": np.ascontiguousarray(
            np.asarray(inputs["L0_ffn_lut"], dtype=np.float32).reshape(1, 16)),
        "gidxT": np.ascontiguousarray(np.asarray(inputs["L0_ffn_gate_idx"]).T.astype(np.int32)),
        "uidxT": np.ascontiguousarray(np.asarray(inputs["L0_ffn_up_idx"]).T.astype(np.int32)),
        "didxT": np.ascontiguousarray(np.asarray(inputs["L0_ffn_down_idx"]).T.astype(np.int32)),
    }
    nc = build_program()
    in_maps = []
    for c in range(8):
        m = dict(base)
        m["xT"] = np.ascontiguousarray(x[c].T)
        in_maps.append(m)
    res = run_bass_kernel_spmd(nc, in_maps, core_ids=list(range(8)))
    out = np.stack([np.asarray(res.results[c]["outT"]).T for c in range(8)])
    return out.astype(np.float32)


# version 06
# speedup vs baseline: 1.3329x; 1.1112x over previous
import math
import numpy as np
import concourse.bass as bass
import concourse.mybir as mybir
from concourse.bass_utils import run_bass_kernel_spmd
from concourse.alu_op_type import AluOpType as OP
from contextlib import ExitStack

F32 = mybir.dt.float32
BF16 = mybir.dt.bfloat16
I32 = mybir.dt.int32
AF = mybir.ActivationFunctionType

T = 2048
D = 2048
H = 8
DK = 256
DV = 512
HDV = H * DV
INTER = 5504
NKT = D // 128
MT = INTER // 128
EPS = 1e-5
ROPE_BASE = 10000.0
SCALE = DK ** -0.5
GAMMAS = [1.0 - 2.0 ** (-5.0 - h) for h in range(H)]
PI = math.pi

ENGS = ("SY", "T", "S", "V", "G")
SEMS = {"T": ("T", 1), "S": ("S", 1), "V": ("V", 1), "G": ("G", 1),
        "SYd": ("SY", 16), "Gd": ("G", 16), "Out": ("SY", 16)}


class Slot:
    __slots__ = ("ready", "free")

    def __init__(self):
        self.ready = None
        self.free = None


class B:
    def __init__(self, nc):
        self.nc = nc
        self.streams = {e: [] for e in ENGS}
        self.sems = dict(SEMS)
        self.cnt = {s: 0 for s in SEMS}
        self.semobj = {}
        self._stack = None

    def add_sem(self, name):
        if name not in self.sems:
            self.sems[name] = (None, 16)
            self.cnt[name] = 0
            self.semobj[name] = self._stack.enter_context(
                self.nc.semaphore(f"sem_{name}"))
        return name

    def op(self, st, fn, ev=True):
        inc = None
        if ev:
            self.cnt[st] += 1
            inc = (st, 1)
        self.streams[st].append(("i", fn, inc))
        return (st, self.cnt[st]) if ev else None

    def dma(self, st, out, in_, sem=None):
        sem = sem or ("Gd" if st == "G" else "SYd")
        self.cnt[sem] += 16
        self.streams[st].append(
            ("i", lambda e, _o=out, _i=in_: e.dma_start(out=_o, in_=_i), (sem, 16)))
        return (sem, self.cnt[sem])

    def wait(self, st, ev):
        if ev is None:
            return
        sem, val = ev
        if self.sems[sem][0] == st:
            return
        self.streams[st].append(("w", sem, val))

    def total(self, sem):
        return (sem, self.cnt[sem])

    def wait_force(self, st, ev):
        if ev is None:
            return
        sem, val = ev
        self.streams[st].append(("w", sem, val))

    def aw(self, st, slot):
        self.wait(st, slot.free)

    def ar(self, st, slot):
        self.wait(st, slot.ready)

    def flush(self, name):
        nc = self.nc
        streams = {e: list(self.streams[e]) for e in ENGS}
        for e in ENGS:
            self.streams[e] = []
        if not any(streams.values()):
            return

        def mk(e):
            def run(eng):
                for it in streams[e]:
                    if it[0] == "w":
                        eng.wait_ge(self.semobj[it[1]], it[2])
                    else:
                        ins = it[1](eng)
                        if it[2] is not None:
                            ins.then_inc(self.semobj[it[2][0]], it[2][1])
            return run

        with nc.Block(name) as blk:
            if streams["SY"]:
                blk.sync(mk("SY"))
            if streams["T"]:
                blk.tensor(mk("T"))
            if streams["S"]:
                blk.scalar(mk("S"))
            if streams["V"]:
                blk.vector(mk("V"))
            if streams["G"]:
                blk.gpsimd(mk("G"))


def build_program(dbg=()):
    nc = bass.Bass("TRN2", target_bir_lowering=False)
    b = B(nc)
    dk = lambda n: "ExternalOutput" if n in dbg else "Internal"

    xT = nc.dram_tensor("xT", [D, T], F32, kind="ExternalInput").ap()
    wq = nc.dram_tensor("wq", [D, D], F32, kind="ExternalInput").ap()
    wk = nc.dram_tensor("wk", [D, D], F32, kind="ExternalInput").ap()
    wv = nc.dram_tensor("wv", [D, HDV], F32, kind="ExternalInput").ap()
    wgt = nc.dram_tensor("wgt", [D, HDV], F32, kind="ExternalInput").ap()
    wo = nc.dram_tensor("wo", [HDV, D], F32, kind="ExternalInput").ap()
    gn = nc.dram_tensor("gn", [DV, 1], F32, kind="ExternalInput").ap()
    lut = nc.dram_tensor("lut", [1, 16], F32, kind="ExternalInput").ap()
    gidxT = nc.dram_tensor("gidxT", [D, INTER], I32, kind="ExternalInput").ap()
    uidxT = nc.dram_tensor("uidxT", [D, INTER], I32, kind="ExternalInput").ap()
    didxT = nc.dram_tensor("didxT", [INTER, D], I32, kind="ExternalInput").ap()
    outT = nc.dram_tensor("outT", [D, T], F32, kind="ExternalOutput").ap()

    qT_s = nc.dram_tensor("qT_s", [D, T], BF16, kind=dk("qT_s")).ap()
    kT_s = nc.dram_tensor("kT_s", [D, T], BF16, kind=dk("kT_s")).ap()
    v_s = nc.dram_tensor("v_s", [T, HDV], BF16, kind=dk("v_s")).ap()
    gT_s = nc.dram_tensor("gT_s", [HDV, T], BF16, kind=dk("gT_s")).ap()
    x2_s = nc.dram_tensor("x2_s", [D, T], F32, kind=dk("x2_s")).ap()
    wg_s = nc.dram_tensor("wg_s", [D, INTER], BF16, kind=dk("wg_s")).ap()
    wu_s = nc.dram_tensor("wu_s", [D, INTER], BF16, kind=dk("wu_s")).ap()
    wd_s = nc.dram_tensor("wd_s", [INTER, D], BF16, kind=dk("wd_s")).ap()

    top = ExitStack()
    sbt = lambda n, s, d: top.enter_context(nc.sbuf_tensor(n, s, d))

    with top:
        b._stack = top
        for s in SEMS:
            b.semobj[s] = top.enter_context(nc.semaphore(f"sem_{s}"))

        lutrep = sbt("lutrep", [128, 16], F32)
        gncol = sbt("gncol", [128, 4], F32)
        ones_bf = sbt("ones_bf", [128, 1], BF16)
        onesrow_f = sbt("onesrow_f", [1, 128], F32)
        epscol = sbt("epscol", [128, 1], F32)
        gidxb = [sbt(f"gidxb{i}", [128, 2048], BF16) for i in range(3)]
        gacc = [sbt(f"gacc{i}", [128, 2048], BF16) for i in range(2)]
        gtmp = sbt("gtmp", [128, 2048], BF16)

        pb = [top.enter_context(nc.psum_tensor(f"pb{i}", [128, 512], F32))
              for i in range(8)]
        pbs = [Slot() for _ in range(8)]

        gidx_slots = [Slot(), Slot(), Slot()]
        gacc_slots = [Slot(), Slot()]
        GCH = []
        for mat, src, dst in (("wg", gidxT, wg_s), ("wu", uidxT, wu_s)):
            for k in range(NKT):
                for c0 in range(0, INTER, 2048):
                    GCH.append((mat, src, dst, k, c0, min(2048, INTER - c0)))
        for k2 in range(MT):
            GCH.append(("wd", didxT, wd_s, k2, 0, 2048))
        G_DONE = {}
        gq = iter(range(len(GCH)))

        for _m in ("wg", "wu", "wd"):
            for _i in range(3):
                b.add_sem(f"Sg_{_m}{_i}")
        for _i in range(3):
            b.add_sem(f"Gi{_i}")

        def tick(n=1):
            for _ in range(n):
                ci = next(gq, None)
                if ci is None:
                    return
                mat, src, dst, k, c0, w = GCH[ci]
                sl = ci % 3
                al = ci % 2
                b.aw("G", gidx_slots[sl])
                e_ld = b.dma("G", gidxb[sl][:, 0:w],
                             src[k * 128:(k + 1) * 128, c0:c0 + w],
                             sem=f"Gi{sl}")
                gidx_slots[sl].ready = e_ld
                b.ar("V", gidx_slots[sl])
                b.aw("V", gacc_slots[al])
                e_acc = None
                for vv in range(16):
                    b.op("V", lambda e, _s=sl, _w=w, _v=vv: e.tensor_scalar(
                        gtmp[:, 0:_w], gidxb[_s][:, 0:_w], float(_v),
                        lutrep[:, _v:_v + 1], OP.is_equal, OP.mult), ev=False)
                    if vv == 0:
                        b.op("V", lambda e, _a=al, _w=w: e.tensor_copy(
                            gacc[_a][:, 0:_w], gtmp[:, 0:_w]), ev=False)
                    else:
                        e_acc = b.op("V", lambda e, _a=al, _w=w: e.tensor_tensor(
                            gacc[_a][:, 0:_w], gacc[_a][:, 0:_w], gtmp[:, 0:_w],
                            OP.add), ev=(vv == 15))
                gidx_slots[sl].free = e_acc
                gacc_slots[al].ready = e_acc
                b.ar("G", gacc_slots[al])
                e_sp = b.dma("G", dst[k * 128:(k + 1) * 128, c0:c0 + w],
                             gacc[al][:, 0:w], sem=f"Sg_{mat}{al}")
                gacc_slots[al].free = e_sp
                G_DONE[mat] = tuple(b.total(f"Sg_{mat}{i}") for i in range(2))

        e_const = [None]

        def ln_stats(src_dram, mean_row, var_row, srow, scope, tag):
            s_ = lambda n, s, d: scope.enter_context(nc.sbuf_tensor(n, s, d))
            xf = [s_(f"lsxf{tag}{i}", [128, T], F32) for i in range(2)]
            xbf = [s_(f"lsxb{tag}{i}", [128, T], BF16) for i in range(2)]
            sqbf = [s_(f"lssq{tag}{i}", [128, T], BF16) for i in range(2)]
            b.add_sem(f"Lx{tag}0")
            b.add_sem(f"Lx{tag}1")
            xfs = [Slot(), Slot()]
            xbs = [Slot(), Slot()]
            e_last = None
            for k in range(NKT):
                sl = k % 2
                b.aw("SY", xfs[sl])
                e_x = b.dma("SY", xf[sl][:, :], src_dram[k * 128:(k + 1) * 128, :],
                            sem=f"Lx{tag}{sl}")
                xfs[sl].ready = e_x
                b.ar("S", xfs[sl])
                b.aw("S", xbs[sl])
                b.op("S", lambda e, _s=sl: e.copy(xbf[_s][:, :], xf[_s][:, :]),
                     ev=False)
                e_sq = b.op("S", lambda e, _s=sl: e.activation(
                    sqbf[_s][:, :], xf[_s][:, :], AF.Square))
                xfs[sl].free = e_sq
                xbs[sl].ready = e_sq
                b.ar("T", xbs[sl])
                if k == 0:
                    b.wait("T", e_const[0])
                    for n in range(8):
                        b.aw("T", pbs[n])
                for n in range(4):
                    b.op("T", lambda e, _n=n, _s=sl, _k=k: e.matmul(
                        pb[_n][0:1, :], ones_bf[:, :],
                        xbf[_s][:, _n * 512:(_n + 1) * 512],
                        start=(_k == 0), stop=(_k == NKT - 1)), ev=False)
                for n in range(4):
                    e_mm = b.op("T", lambda e, _n=n, _s=sl, _k=k: e.matmul(
                        pb[4 + _n][0:1, :], ones_bf[:, :],
                        sqbf[_s][:, _n * 512:(_n + 1) * 512],
                        start=(_k == 0), stop=(_k == NKT - 1)), ev=(n == 3))
                xbs[sl].free = e_mm
                e_last = e_mm
                if k % 2 == 1:
                    tick(1)
            for n in range(8):
                pbs[n].ready = e_last
            b.wait("S", e_last)
            for n in range(4):
                b.op("S", lambda e, _n=n: e.activation(
                    mean_row[0:1, _n * 512:(_n + 1) * 512], pb[_n][0:1, :],
                    AF.Copy, scale=1.0 / D), ev=False)
            for n in range(4):
                e_ev = b.op("S", lambda e, _n=n: e.activation(
                    var_row[0:1, _n * 512:(_n + 1) * 512], pb[4 + _n][0:1, :],
                    AF.Copy, scale=1.0 / D), ev=(n == 3))
            for n in range(8):
                pbs[n].free = e_ev
            b.wait("V", e_ev)
            b.op("V", lambda e: e.tensor_tensor(srow[0:1, :], mean_row[0:1, :],
                                                mean_row[0:1, :], OP.mult), ev=False)
            e_var = b.op("V", lambda e: e.tensor_tensor(
                var_row[0:1, :], var_row[0:1, :], srow[0:1, :], OP.subtract))
            b.wait("S", e_var)
            e_sd = b.op("S", lambda e: e.activation(
                var_row[0:1, :], var_row[0:1, :], AF.Sqrt, bias=epscol[0:1, 0:1]))
            b.wait("V", e_sd)
            e_istd = b.op("V", lambda e: e.reciprocal(var_row[0:1, :],
                                                      var_row[0:1, :]))
            return e_istd

        def bcast_rows(pairs, e_dep, tag):
            sm = b.add_sem(f"Bc{tag}")
            b.wait("SY", e_dep)
            e_rep = None
            for (row, rep) in pairs:
                if e_rep is not None:
                    b.wait("SY", e_rep)
                e_rep = b.dma("SY", rep[0:1, :], row[0:1, :], sem=sm)
                n = 1
                while n < 128:
                    b.wait("SY", e_rep)
                    e_rep = b.dma("SY", rep[n:min(2 * n, 128), :], rep[0:n, :],
                                  sem=sm)
                    n *= 2
            return e_rep

        sc2 = ExitStack()
        with sc2:
            cos_t = sc2.enter_context(nc.sbuf_tensor("cos_t", [128, T], F32))
            sin_t = sc2.enter_context(nc.sbuf_tensor("sin_t", [128, T], F32))
            xln = sc2.enter_context(nc.sbuf_tensor("xln", [128, NKT * T], BF16))

            p0 = ExitStack()
            s0 = lambda n, s, d: p0.enter_context(nc.sbuf_tensor(n, s, d))
            with p0:
                iot = s0("iot", [128, T], F32)
                us = s0("us", [128, T], F32)
                uc = s0("uc", [128, T], F32)
                ui = s0("ui", [128, T], I32)
                uf = s0("uf", [128, T], F32)
                invf = s0("invf", [128, 1], F32)
                lnb = s0("lnb", [128, 1], F32)
                lrow = s0("lrow", [1, 16], F32)

                b.add_sem("SYl")
                b.add_sem("SYm")
                e_lut = b.dma("SY", lrow[0:1, :], lut[0:1, :], sem="SYl")
                for c in range(4):
                    e_gnl = b.dma("SY", gncol[:, c:c + 1],
                                  gn[c * 128:(c + 1) * 128, 0:1], sem="SYm")

                b.op("G", lambda e: e.memset(ones_bf[:, :], 1.0), ev=False)
                b.op("G", lambda e: e.memset(onesrow_f[:, :], 1.0), ev=False)
                b.op("G", lambda e: e.memset(epscol[:, :], EPS), ev=False)
                b.op("G", lambda e: e.memset(lnb[:, :], -math.log(2 * PI)), ev=False)
                b.op("G", lambda e: e.iota(iot[:, :], [[1, T]], channel_multiplier=0,
                                           allow_small_or_imprecise_dtypes=True),
                     ev=False)
                e_pc = b.op("G", lambda e: e.iota(
                    invf[:, :], [[0, 1]], channel_multiplier=1,
                    allow_small_or_imprecise_dtypes=True))
                e_const[0] = e_pc
                b.add_sem("Glut")
                b.wait("G", e_lut)
                e_lr = b.dma("G", lutrep[0:1, :], lrow[0:1, :], sem="Glut")
                n = 1
                while n < 128:
                    b.wait("G", e_lr)
                    e_lr = b.dma("G", lutrep[n:min(2 * n, 128), :], lutrep[0:n, :],
                                 sem="Glut")
                    n *= 2
                b.wait("V", e_lr)

                b.wait("S", e_pc)
                e_invf = b.op("S", lambda e: e.activation(
                    invf[:, :], invf[:, :], AF.Exp,
                    scale=-math.log(ROPE_BASE) / 128.0, bias=lnb[:, 0:1]))
                b.wait("V", e_invf)
                b.op("V", lambda e: e.tensor_scalar(
                    us[:, :], iot[:, :], invf[:, 0:1], None, OP.mult), ev=False)
                b.op("V", lambda e: e.tensor_scalar(
                    uc[:, :], us[:, :], 0.25, None, OP.add), ev=False)
                b.op("V", lambda e: e.tensor_copy(ui[:, :], us[:, :]), ev=False)
                b.op("V", lambda e: e.tensor_copy(uf[:, :], ui[:, :]), ev=False)
                e_us = b.op("V", lambda e: e.tensor_tensor(
                    us[:, :], us[:, :], uf[:, :], OP.subtract))
                b.op("V", lambda e: e.tensor_copy(ui[:, :], uc[:, :]), ev=False)
                b.op("V", lambda e: e.tensor_copy(uf[:, :], ui[:, :]), ev=False)
                e_uc = b.op("V", lambda e: e.tensor_tensor(
                    uc[:, :], uc[:, :], uf[:, :], OP.subtract))
                b.wait("S", e_us)
                b.op("S", lambda e: e.activation(sin_t[:, :], us[:, :], AF.Sin,
                                                 scale=2 * PI), ev=False)
                b.wait("S", e_uc)
                b.op("S", lambda e: e.activation(cos_t[:, :], uc[:, :], AF.Sin,
                                                 scale=2 * PI), ev=False)
                b.flush("p0")

            lnA = ExitStack()
            with lnA:
                mrep1 = lnA.enter_context(nc.sbuf_tensor("mrep1", [128, T], F32))
                irep1 = lnA.enter_context(nc.sbuf_tensor("irep1", [128, T], F32))
                p05 = ExitStack()
                with p05:
                    mean1 = p05.enter_context(nc.sbuf_tensor("mean1", [1, T], F32))
                    var1 = p05.enter_context(nc.sbuf_tensor("var1", [1, T], F32))
                    srow1 = p05.enter_context(nc.sbuf_tensor("srow1", [1, T], F32))
                    e_istd = ln_stats(xT, mean1, var1, srow1, p05, 'a')
                    e_ln1 = bcast_rows(((mean1, mrep1), (var1, irep1)), e_istd, 'a')
                    if "ln1" in dbg:
                        mean_d = nc.dram_tensor("mean_d", [1, T], F32,
                                                kind="ExternalOutput").ap()
                        var_d = nc.dram_tensor("var_d", [1, T], F32,
                                               kind="ExternalOutput").ap()
                        mrep_d = nc.dram_tensor("mrep_d", [128, T], F32,
                                                kind="ExternalOutput").ap()
                        irep_d = nc.dram_tensor("irep_d", [128, T], F32,
                                                kind="ExternalOutput").ap()
                        b.wait("SY", e_ln1)
                        b.dma("SY", mean_d[0:1, :], mean1[0:1, :])
                        b.dma("SY", var_d[0:1, :], var1[0:1, :])
                        b.dma("SY", mrep_d[:, :], mrep1[:, :])
                        b.dma("SY", irep_d[:, :], irep1[:, :])
                    b.flush("p05")

                p1 = ExitStack()
                s1 = lambda n, s, d: p1.enter_context(nc.sbuf_tensor(n, s, d))
                with p1:
                    xf1 = [s1(f"xf1_{i}", [128, T], F32) for i in range(2)]
                    b.add_sem("P1x0")
                    b.add_sem("P1x1")
                    x1s = [Slot(), Slot()]
                    for k in range(NKT):
                        sl = k % 2
                        b.aw("SY", x1s[sl])
                        e_x = b.dma("SY", xf1[sl][:, :], xT[k * 128:(k + 1) * 128, :],
                                    sem=f"P1x{sl}")
                        x1s[sl].ready = e_x
                        b.ar("G", x1s[sl])
                        if k == 0:
                            b.wait("G", e_ln1)
                        b.op("G", lambda e, _s=sl: e.tensor_tensor(
                            xf1[_s][:, :], xf1[_s][:, :], mrep1[:, :], OP.subtract),
                            ev=False)
                        e_g = b.op("G", lambda e, _s=sl: e.tensor_tensor(
                            xf1[_s][:, :], xf1[_s][:, :], irep1[:, :], OP.mult))
                        b.wait("S", e_g)
                        e_c = b.op("S", lambda e, _s=sl, _k=k: e.copy(
                            xln[:, _k * T:(_k + 1) * T], xf1[_s][:, :]))
                        x1s[sl].free = e_c
                        if k % 2 == 1:
                            tick(1)
                    xln_ev = e_c
                    b.flush("p1")

            p2 = ExitStack()
            s2 = lambda n, s, d: p2.enter_context(nc.sbuf_tensor(n, s, d))
            with p2:
                wbuf = [s2(f"wbuf{i}", [128, NKT * 1024], BF16) for i in range(2)]
                stg = [s2(f"stg{i}", [128, 1024], F32) for i in range(2)]
                ea = s2("ea", [128, T], F32)
                ebn = s2("ebn", [128, 512], F32)
                r1n = s2("r1n", [128, 512], F32)
                r2n = s2("r2n", [128, 512], F32)
                rtn = s2("rtn", [128, 512], F32)
                rb2 = s2("rb2", [128, T], BF16)
                gsl = s2("gsl", [128, T], BF16)
                rb1 = gsl
                vsl = s2("vsl", [128, 1024], BF16)
                wbs = [Slot(), Slot()]
                stgs = [Slot(), Slot()]
                fill_idx = [0]
                eas = Slot()
                ebs = Slot()
                rns = Slot()
                ros = Slot()
                gss = Slot()
                vss = Slot()
                b.wait("T", xln_ev)

                b.add_sem("St0")
                b.add_sem("St1")
                b.add_sem("SYq")
                b.add_sem("SYgs")
                b.add_sem("SYvs")

                def load_half(wsrc, half):
                    wi = fill_idx[0] % 2
                    fill_idx[0] += 1
                    b.aw("SY", wbs[wi])
                    e_cast = None
                    for k in range(NKT):
                        ssl = k % 2
                        b.aw("SY", stgs[ssl])
                        e_d = b.dma("SY", stg[ssl][:, :],
                                    wsrc[k * 128:(k + 1) * 128,
                                         half * 1024:(half + 1) * 1024],
                                    sem=f"St{ssl}")
                        stgs[ssl].ready = e_d
                        b.ar("S", stgs[ssl])
                        e_cast = b.op("S", lambda e, _w=wi, _k=k, _s=ssl: e.copy(
                            wbuf[_w][:, _k * 1024:(_k + 1) * 1024], stg[_s][:, :]))
                        stgs[ssl].free = e_cast
                    wbs[wi].ready = e_cast
                    b.ar("T", wbs[wi])
                    return wi

                for wn, wsrc, wdst in (("q", wq, qT_s), ("k", wk, kT_s)):
                    for half in range(2):
                        wi = load_half(wsrc, half)
                        for mrel in range(8):
                            m = half * 8 + mrel
                            grp = (mrel % 2) * 4
                            for n in range(4):
                                b.aw("T", pbs[grp + n])
                            e_mm = None
                            for k in range(NKT):
                                for n in range(4):
                                    e_mm = b.op("T", lambda e, _g=grp, _n=n, _k=k,
                                                _m=mrel, _w=wi: e.matmul(
                                                    pb[_g + _n][:, :],
                                                    wbuf[_w][:, _k * 1024 + _m * 128:
                                                         _k * 1024 + (_m + 1) * 128],
                                                    xln[:, _k * T + _n * 512:
                                                        _k * T + (_n + 1) * 512],
                                                    start=(_k == 0),
                                                    stop=(_k == NKT - 1)),
                                                ev=(k == NKT - 1 and n == 3))
                            for n in range(4):
                                pbs[grp + n].ready = e_mm
                            h = m // 2
                            if m % 2 == 0:
                                b.aw("S", eas)
                                b.wait("S", e_mm)
                                for n in range(4):
                                    e_ev = b.op("S", lambda e, _g=grp, _n=n: e.copy(
                                        ea[:, _n * 512:(_n + 1) * 512],
                                        pb[_g + _n][:, :]), ev=(n == 3))
                                for n in range(4):
                                    pbs[grp + n].free = e_ev
                                eas.ready = e_ev
                            else:
                                b.ar("G", eas)
                                b.aw("G", ros)
                                e_rp = None
                                for n in range(4):
                                    b.ar("S", pbs[grp + n])
                                    b.aw("S", ebs)
                                    e_eb = b.op("S", lambda e, _g=grp, _n=n: e.copy(
                                        ebn[:, :], pb[_g + _n][:, :]))
                                    pbs[grp + n].free = e_eb
                                    ebs.ready = e_eb
                                    b.ar("G", ebs)
                                    b.aw("G", rns)
                                    c0 = n * 512
                                    b.op("G", lambda e, _c=c0: e.tensor_tensor(
                                        r1n[:, :], ea[:, _c:_c + 512],
                                        cos_t[:, _c:_c + 512], OP.mult), ev=False)
                                    b.op("G", lambda e, _c=c0: e.tensor_tensor(
                                        rtn[:, :], ebn[:, :],
                                        sin_t[:, _c:_c + 512], OP.mult), ev=False)
                                    b.op("G", lambda e: e.tensor_tensor(
                                        r1n[:, :], r1n[:, :], rtn[:, :],
                                        OP.subtract), ev=False)
                                    b.op("G", lambda e, _c=c0: e.tensor_tensor(
                                        r2n[:, :], ebn[:, :],
                                        cos_t[:, _c:_c + 512], OP.mult), ev=False)
                                    b.op("G", lambda e, _c=c0: e.tensor_tensor(
                                        rtn[:, :], ea[:, _c:_c + 512],
                                        sin_t[:, _c:_c + 512], OP.mult), ev=False)
                                    e_rp = b.op("G", lambda e: e.tensor_tensor(
                                        r2n[:, :], r2n[:, :], rtn[:, :], OP.add))
                                    ebs.free = e_rp
                                    rns.ready = e_rp
                                    b.ar("S", rns)
                                    b.op("S", lambda e, _c=c0: e.copy(
                                        rb1[:, _c:_c + 512], r1n[:, :]), ev=False)
                                    e_cb = b.op("S", lambda e, _c=c0: e.copy(
                                        rb2[:, _c:_c + 512], r2n[:, :]))
                                    rns.free = e_cb
                                eas.free = e_rp
                                ros.ready = e_cb
                                b.ar("SY", ros)
                                r0 = h * 256
                                b.dma("SY", wdst[r0:r0 + 128, :], rb1[:, :],
                                      sem="SYq")
                                e_sp = b.dma("SY", wdst[r0 + 128:r0 + 256, :],
                                             rb2[:, :], sem="SYq")
                                ros.free = e_sp
                                tick(1)
                        wbs[wi].free = e_mm

                for half in range(4):
                    wi = load_half(wgt, half)
                    for mrel in range(8):
                        m = half * 8 + mrel
                        grp = (mrel % 2) * 4
                        for n in range(4):
                            b.aw("T", pbs[grp + n])
                        e_mm = None
                        for k in range(NKT):
                            for n in range(4):
                                e_mm = b.op("T", lambda e, _g=grp, _n=n, _k=k,
                                            _m=mrel, _w=wi: e.matmul(
                                                pb[_g + _n][:, :],
                                                wbuf[_w][:, _k * 1024 + _m * 128:
                                                     _k * 1024 + (_m + 1) * 128],
                                                xln[:, _k * T + _n * 512:
                                                    _k * T + (_n + 1) * 512],
                                                start=(_k == 0),
                                                stop=(_k == NKT - 1)),
                                            ev=(k == NKT - 1 and n == 3))
                        for n in range(4):
                            pbs[grp + n].ready = e_mm
                        b.aw("S", gss)
                        b.wait("S", e_mm)
                        for n in range(4):
                            e_ev = b.op("S", lambda e, _g=grp, _n=n: e.copy(
                                gsl[:, _n * 512:(_n + 1) * 512],
                                pb[_g + _n][:, :]), ev=(n == 3))
                        for n in range(4):
                            pbs[grp + n].free = e_ev
                        gss.ready = e_ev
                        b.ar("SY", gss)
                        e_sp = b.dma("SY", gT_s[m * 128:(m + 1) * 128, :],
                                     gsl[:, :], sem="SYgs")
                        gss.free = e_sp
                        tick(1)
                    wbs[wi].free = e_mm

                for cgrp in range(4):
                    wi = load_half(wv, cgrp)
                    for t in range(NKT):
                        grp = (t % 2) * 2
                        b.aw("T", pbs[grp])
                        b.aw("T", pbs[grp + 1])
                        e_mm = None
                        for k in range(NKT):
                            for c in range(2):
                                e_mm = b.op("T", lambda e, _g=grp, _c=c, _k=k, _t=t,
                                            _w=wi: e.matmul(
                                                pb[_g + _c][:, :],
                                                xln[:, _k * T + _t * 128:
                                                    _k * T + (_t + 1) * 128],
                                                wbuf[_w][:, _k * 1024 + _c * 512:
                                                     _k * 1024 + (_c + 1) * 512],
                                                start=(_k == 0),
                                                stop=(_k == NKT - 1)),
                                            ev=(k == NKT - 1 and c == 1))
                        pbs[grp].ready = e_mm
                        pbs[grp + 1].ready = e_mm
                        b.aw("S", vss)
                        b.wait("S", e_mm)
                        for c in range(2):
                            e_ev = b.op("S", lambda e, _g=grp, _c=c: e.copy(
                                vsl[:, _c * 512:(_c + 1) * 512],
                                pb[_g + _c][:, :]), ev=(c == 1))
                        pbs[grp].free = e_ev
                        pbs[grp + 1].free = e_ev
                        vss.ready = e_ev
                        b.ar("SY", vss)
                        e_sp = b.dma("SY", v_s[t * 128:(t + 1) * 128,
                                               cgrp * 1024:(cgrp + 1) * 1024],
                                     vsl[:, :], sem="SYvs")
                        vss.free = e_sp
                        if t % 2 == 1:
                            tick(1)
                    wbs[wi].free = e_mm
                b.flush("p2")

        p2_spills = [b.total("SYq"), b.total("SYgs"), b.total("SYvs")]
        p3 = ExitStack()
        s3 = lambda n, s, d: p3.enter_context(nc.sbuf_tensor(n, s, d))
        with p3:
            dmat = [s3(f"dmat{h}", [128, 512], BF16) for h in range(H)]
            masks = [s3(f"mask{i}", [128, 512], BF16) for i in range(4)]
            dio = s3("dio", [128, 512], F32)
            khb = [s3(f"khb{i}", [128, 2 * T], BF16) for i in range(2)]
            qhb = [s3(f"qhb{i}", [128, 1024], BF16) for i in range(2)]
            vhb = [s3(f"vhb{i}", [128, 512], BF16) for i in range(6)]
            ghb = [s3(f"ghb{i}", [128, 4 * 512], BF16) for i in range(2)]
            stf = [s3(f"stf{i}", [128, 512], F32) for i in range(2)]
            stb = [s3(f"stb{i}", [128, 512], BF16) for i in range(6)]
            osq = s3("osq", [128, 512], BF16)
            ofb = [s3(f"ofb{i}", [128, 4 * 512], F32) for i in range(2)]
            rrow = s3("rrow", [1, 512], F32)
            rmsrep = s3("rmsrep", [128, 512], BF16)
            obf = s3("obf", [128, 512], BF16)
            sgb = s3("sgb", [128, 512], BF16)
            zb1 = s3("zb1", [128, 512], BF16)
            zbuf = s3("zbuf", [128, 32 * 512], BF16)
            wosl = [s3(f"wosl{i}", [128, 1024], BF16) for i in range(2)]
            stg3 = [s3(f"stg3_{i}", [128, 1024], F32) for i in range(2)]
            attf = s3("attf", [128, 512], F32)
            xsl = [s3(f"xsl{i}", [128, 512], F32) for i in range(2)]
            x2f = [s3(f"x2f{i}", [128, 512], F32) for i in range(2)]
            khs = [Slot(), Slot()]
            qhs = [Slot(), Slot()]
            vhs = [Slot() for _ in range(6)]
            ghs = [Slot(), Slot()]
            stfs = [Slot(), Slot()]
            stbs = [Slot() for _ in range(6)]
            osqs = Slot()
            ofbs = [Slot(), Slot()]
            rrs = Slot()
            rmss = Slot()
            obfs = Slot()
            sgs = Slot()
            z1s = Slot()
            zbufs = Slot()
            wosls = [Slot(), Slot()]
            stg3s = [Slot(), Slot()]
            attfs = Slot()
            b.add_sem("St2")
            b.add_sem("St3")
            xsls = [Slot(), Slot()]
            x2fs = [Slot(), Slot()]

            for _s in ("K0", "K1", "Q0", "Q1", "Gg0", "Gg1", "X0", "X1",
                       "XS0", "XS1", "W0", "W1"):
                b.add_sem(_s)
            for _s in (f"Vt{i}" for i in range(6)):
                b.add_sem(_s)
            for _e in p2_spills:
                b.wait("SY", _e)
            b.wait("S", b.total("SYm"))
            e_dio = b.op("G", lambda e: e.iota(
                dio[:, :], [[1, 512]], channel_multiplier=-1,
                allow_small_or_imprecise_dtypes=True))
            b.wait("S", e_dio)
            e_dm = None
            for h in range(H):
                e_dm = b.op("S", lambda e, _h=h: e.activation(
                    dmat[_h][:, :], dio[:, :], AF.Exp,
                    scale=math.log(GAMMAS[_h])))
            for i in range(4):
                e_mio = b.op("G", lambda e, _i=i: e.iota(
                    dio[:, :], [[1, 512]], base=-128 * _i, channel_multiplier=-1,
                    allow_small_or_imprecise_dtypes=True) if _i > 0 else e.nop())
                if i == 0:
                    b.wait("G", e_dm)
                b.wait("V", e_mio)
                e_msk = b.op("V", lambda e, _i=i: e.tensor_scalar(
                    masks[_i][:, :], dio[:, :], -0.5, None, OP.is_gt))
                if i < 3:
                    b.wait("G", e_msk)
            for isb in range(4):
                njt = 4 * isb + 4
                for h in range(H):
                    hp = h % 2
                    b.aw("SY", khs[hp])
                    b.dma("SY", khb[hp][:, 0:T], kT_s[h * 256:h * 256 + 128, :],
                          sem=f"K{hp}")
                    e_k = b.dma("SY", khb[hp][:, T:2 * T],
                                kT_s[h * 256 + 128:h * 256 + 256, :], sem=f"K{hp}")
                    khs[hp].ready = e_k
                    b.aw("SY", qhs[hp])
                    b.dma("SY", qhb[hp][:, 0:512],
                          qT_s[h * 256:h * 256 + 128, isb * 512:(isb + 1) * 512],
                          sem=f"Q{hp}")
                    e_q = b.dma("SY", qhb[hp][:, 512:1024],
                                qT_s[h * 256 + 128:h * 256 + 256,
                                     isb * 512:(isb + 1) * 512], sem=f"Q{hp}")
                    qhs[hp].ready = e_q
                    b.aw("SY", ghs[hp])
                    e_g = None
                    for c in range(4):
                        e_g = b.dma("SY", ghb[hp][:, c * 512:(c + 1) * 512],
                                    gT_s[h * 512 + c * 128:h * 512 + (c + 1) * 128,
                                         isb * 512:(isb + 1) * 512], sem=f"Gg{hp}")
                    ghs[hp].ready = e_g

                    b.ar("T", khs[hp])
                    b.ar("T", qhs[hp])
                    for n in range(4):
                        b.aw("T", pbs[2 + n])
                    e_omm = None
                    e_smm = None
                    st_ev = {}

                    def emit_score(jt):
                        nonlocal e_smm
                        vsl_ = jt % 6
                        b.aw("SY", vhs[vsl_])
                        e_v = b.dma("SY", vhb[vsl_][:, :],
                                    v_s[jt * 128:(jt + 1) * 128,
                                        h * 512:(h + 1) * 512], sem=f"Vt{vsl_}")
                        vhs[vsl_].ready = e_v
                        sbk = jt % 2
                        b.aw("T", pbs[sbk])
                        for c in range(2):
                            e_smm = b.op("T", lambda e, _b=sbk, _c=c, _j=jt, _p=hp:
                                         e.matmul(pb[_b][:, :],
                                                  khb[_p][:, _c * T + _j * 128:
                                                       _c * T + (_j + 1) * 128],
                                                  qhb[_p][:, _c * 512:(_c + 1) * 512],
                                                  start=(_c == 0), stop=(_c == 1)),
                                         ev=(c == 1))
                        pbs[sbk].ready = e_smm
                        ctile = SCALE * (GAMMAS[h] ** (isb * 512 - jt * 128))
                        esl = jt % 2
                        b.ar("S", pbs[sbk])
                        b.aw("S", stfs[esl])
                        e_sev = b.op("S", lambda e, _b=sbk, _c=ctile, _e=esl:
                                     e.activation(stf[_e][:, :], pb[_b][:, :],
                                                  AF.Copy, scale=_c))
                        pbs[sbk].free = e_sev
                        stfs[esl].ready = e_sev
                        ssl = jt % 6
                        b.ar("V", stfs[esl])
                        b.aw("V", stbs[ssl])
                        doff = jt - 4 * isb
                        if doff >= 0:
                            b.op("V", lambda e, _e=esl, _d=doff: e.tensor_tensor(
                                stf[_e][:, :], stf[_e][:, :], masks[_d][:, :],
                                OP.mult), ev=False)
                        e_dec = b.op("V", lambda e, _e=esl, _s=ssl, _h=h:
                                     e.tensor_tensor(stb[_s][:, :], stf[_e][:, :],
                                                     dmat[_h][:, :], OP.mult))
                        stfs[esl].free = e_dec
                        stbs[ssl].ready = e_dec

                    def emit_o(jt):
                        nonlocal e_omm
                        ssl = jt % 6
                        vsl_ = jt % 6
                        b.ar("T", stbs[ssl])
                        b.ar("T", vhs[vsl_])
                        for dvc in range(4):
                            e_omm = b.op("T", lambda e, _d=dvc, _j=jt, _s=ssl,
                                         _v=vsl_: e.matmul(
                                             pb[2 + _d][:, :],
                                             vhb[_v][:, _d * 128:(_d + 1) * 128],
                                             stb[_s][:, :],
                                             start=(_j == 0), stop=(_j == njt - 1)),
                                         ev=(dvc == 3))
                        stbs[ssl].free = e_omm
                        vhs[vsl_].free = e_omm

                    for jt in range(njt):
                        emit_score(jt)
                        if jt >= 1:
                            emit_o(jt - 1)
                    emit_o(njt - 1)
                    khs[hp].free = e_smm
                    qhs[hp].free = e_smm
                    for n in range(4):
                        pbs[2 + n].ready = e_omm
                    osl_ = h % 2
                    b.aw("S", ofbs[osl_])
                    e_oev = None
                    for dvc in range(4):
                        b.ar("S", pbs[2 + dvc])
                        e_oev = b.op("S", lambda e, _d=dvc, _o=osl_: e.copy(
                            ofb[_o][:, _d * 512:(_d + 1) * 512],
                            pb[2 + _d][:, :]))
                        pbs[2 + dvc].free = e_oev
                    ofbs[osl_].ready = e_oev
                    b.aw("T", pbs[6])
                    e_st = None
                    for dvc in range(4):
                        b.ar("S", ofbs[osl_])
                        b.aw("S", osqs)
                        e_sq = b.op("S", lambda e, _d=dvc, _o=osl_: e.activation(
                            osq[:, :], ofb[_o][:, _d * 512:(_d + 1) * 512],
                            AF.Square))
                        osqs.ready = e_sq
                        b.ar("T", osqs)
                        e_st = b.op("T", lambda e, _d=dvc: e.matmul(
                            pb[6][0:1, :], ones_bf[:, :], osq[:, :],
                            start=(_d == 0), stop=(_d == 3)))
                        osqs.free = e_st
                    pbs[6].ready = e_st
                    b.ar("S", pbs[6])
                    b.aw("S", rrs)
                    e_r1 = b.op("S", lambda e: e.activation(
                        rrow[0:1, :], pb[6][0:1, :], AF.Sqrt,
                        bias=epscol[0:1, 0:1], scale=1.0 / DV))
                    pbs[6].free = e_r1
                    b.wait("V", e_r1)
                    e_r2 = b.op("V", lambda e: e.reciprocal(rrow[0:1, :],
                                                            rrow[0:1, :]))
                    rrs.ready = e_r2
                    b.aw("T", pbs[7])
                    b.ar("T", rrs)
                    e_bc = b.op("T", lambda e: e.matmul(
                        pb[7][:, :], onesrow_f[0:1, :], rrow[0:1, :],
                        start=True, stop=True))
                    rrs.free = e_bc
                    pbs[7].ready = e_bc
                    b.ar("S", pbs[7])
                    b.aw("S", rmss)
                    e_rr = b.op("S", lambda e: e.copy(rmsrep[:, :], pb[7][:, :]))
                    pbs[7].free = e_rr
                    rmss.ready = e_rr
                    b.ar("S", ghs[hp])
                    e_z = None
                    for dvc in range(4):
                        b.aw("S", obfs)
                        e_ob = b.op("S", lambda e, _d=dvc, _o=osl_: e.activation(
                            obf[:, :], ofb[_o][:, _d * 512:(_d + 1) * 512],
                            AF.Copy, scale=gncol[:, _d:_d + 1]))
                        obfs.ready = e_ob
                        b.aw("S", sgs)
                        e_sg = b.op("S", lambda e, _d=dvc, _p=hp: e.activation(
                            sgb[:, :], ghb[_p][:, _d * 512:(_d + 1) * 512],
                            AF.Silu))
                        sgs.ready = e_sg
                        b.ar("V", obfs)
                        b.ar("V", sgs)
                        b.aw("V", z1s)
                        e_z1 = b.op("V", lambda e: e.tensor_tensor(
                            zb1[:, :], obf[:, :], sgb[:, :], OP.mult))
                        obfs.free = e_z1
                        sgs.free = e_z1
                        z1s.ready = e_z1
                        b.ar("V", rmss)
                        k2 = h * 4 + dvc
                        if k2 == 0:
                            b.aw("V", zbufs)
                        e_z = b.op("V", lambda e, _k=k2: e.tensor_tensor(
                            zbuf[:, _k * 512:(_k + 1) * 512], zb1[:, :],
                            rmsrep[:, :], OP.mult))
                        z1s.free = e_z
                        if dvc == 3:
                            rmss.free = e_z
                        if k2 == 31:
                            zbufs.ready = e_z
                    ghs[hp].free = e_z
                    ofbs[osl_].free = e_z
                    tick(1)

                b.ar("T", zbufs)
                e_wmm = None
                for oh in range(2):
                    for k2 in range(32):
                        wsl = k2 % 2
                        b.aw("SY", stg3s[wsl])
                        e_wo = b.dma("SY", stg3[wsl][:, :],
                                     wo[k2 * 128:(k2 + 1) * 128,
                                        oh * 1024:(oh + 1) * 1024],
                                     sem=f"St{2 + wsl}")
                        stg3s[wsl].ready = e_wo
                        b.ar("S", stg3s[wsl])
                        b.aw("S", wosls[wsl])
                        e_wc = b.op("S", lambda e, _w=wsl: e.copy(
                            wosl[_w][:, :], stg3[_w][:, :]))
                        stg3s[wsl].free = e_wc
                        wosls[wsl].ready = e_wc
                        b.ar("T", wosls[wsl])
                        if k2 == 0:
                            for mrel in range(8):
                                b.aw("T", pbs[mrel])
                        for mrel in range(8):
                            e_wmm = b.op("T", lambda e, _m=mrel, _k=k2, _w=wsl:
                                         e.matmul(pb[_m][:, :],
                                                  wosl[_w][:, _m * 128:
                                                       (_m + 1) * 128],
                                                  zbuf[:, _k * 512:(_k + 1) * 512],
                                                  start=(_k == 0), stop=(_k == 31)),
                                         ev=(mrel == 7))
                        wosls[wsl].free = e_wmm
                    for mrel in range(8):
                        pbs[mrel].ready = e_wmm
                    for mrel in range(8):
                        outc = oh * 8 + mrel
                        psl = mrel % 2
                        b.ar("S", pbs[mrel])
                        b.aw("S", attfs)
                        e_at = b.op("S", lambda e, _m=mrel: e.copy(
                            attf[:, :], pb[_m][:, :]))
                        pbs[mrel].free = e_at
                        attfs.ready = e_at
                        b.aw("SY", xsls[psl])
                        e_xl = b.dma("SY", xsl[psl][:, :],
                                     xT[outc * 128:(outc + 1) * 128,
                                        isb * 512:(isb + 1) * 512], sem=f"X{psl}")
                        xsls[psl].ready = e_xl
                        b.ar("G", attfs)
                        b.ar("G", xsls[psl])
                        b.aw("G", x2fs[psl])
                        e_x2 = b.op("G", lambda e, _p=psl: e.tensor_tensor(
                            x2f[_p][:, :], attf[:, :], xsl[_p][:, :], OP.add))
                        attfs.free = e_x2
                        xsls[psl].free = e_x2
                        x2fs[psl].ready = e_x2
                        b.ar("SY", x2fs[psl])
                        e_sp = b.dma("SY", x2_s[outc * 128:(outc + 1) * 128,
                                                isb * 512:(isb + 1) * 512],
                                     x2f[psl][:, :], sem=f"XS{psl}")
                        x2fs[psl].free = e_sp
                zbufs.free = e_wmm
                tick(2)
            x2_done = [b.total("XS0"), b.total("XS1")]
            b.flush("p3")

        lnB = ExitStack()
        with lnB:
            mrep2 = lnB.enter_context(nc.sbuf_tensor("mrep2", [128, T], F32))
            irep2 = lnB.enter_context(nc.sbuf_tensor("irep2", [128, T], F32))
            p35 = ExitStack()
            with p35:
                mean2 = p35.enter_context(nc.sbuf_tensor("mean2", [1, T], F32))
                var2 = p35.enter_context(nc.sbuf_tensor("var2", [1, T], F32))
                srow2 = p35.enter_context(nc.sbuf_tensor("srow2", [1, T], F32))
                for _e in x2_done:
                    b.wait("SY", _e)
                e_istd2 = ln_stats(x2_s, mean2, var2, srow2, p35, 'b')
                e_ln2 = bcast_rows(((mean2, mrep2), (var2, irep2)), e_istd2, 'b')
                b.flush("p35")

            FG = []
            m0 = 0
            while m0 < MT:
                gw = min(2, MT - m0)
                FG.append((m0, gw))
                m0 += gw
            p4 = ExitStack()
            s4 = lambda n, s, d: p4.enter_context(nc.sbuf_tensor(n, s, d))
            with p4:
                x2c = s4("x2c", [128, NKT * 1024], BF16)
                x2p = [s4(f"x2p{i}", [128, 1024], F32) for i in range(2)]
                hbuf = s4("hbuf", [128, MT * 1024], BF16)
                hup = s4("hup", [128, 1024], BF16)
                fsl = [s4(f"fsl{i}", [128, 512], BF16) for i in range(4)]
                yf = [s4(f"yf{i}", [128, 1024], F32) for i in range(2)]
                xr = [s4(f"xr{i}", [128, 1024], F32) for i in range(2)]
                ob = [s4(f"ob{i}", [128, 1024], F32) for i in range(2)]
                for _s in ("P4x0", "P4x1", "F0", "F1", "F2", "F3",
                           "R0", "R1"):
                    b.add_sem(_s)
                x2ps = [Slot(), Slot()]
                x2cs = Slot()
                hbs = [Slot() for _ in range(MT)]
                hups = Slot()
                fsls = [Slot() for _ in range(4)]
                yfs = [Slot(), Slot()]
                xrs = [Slot(), Slot()]
                obs = [Slot(), Slot()]
                n_out = 0
                for tc in range(2):
                    b.aw("G", x2cs)
                    e_c = None
                    for k in range(NKT):
                        sl = k % 2
                        b.aw("SY", x2ps[sl])
                        e_x = b.dma("SY", x2p[sl][:, :],
                                    x2_s[k * 128:(k + 1) * 128,
                                         tc * 1024:(tc + 1) * 1024],
                                    sem=f"P4x{sl}")
                        x2ps[sl].ready = e_x
                        b.ar("G", x2ps[sl])
                        if k == 0 and tc == 0:
                            b.wait("G", e_ln2)
                        b.op("G", lambda e, _s=sl, _t=tc: e.tensor_tensor(
                            x2p[_s][:, :], x2p[_s][:, :],
                            mrep2[:, _t * 1024:(_t + 1) * 1024], OP.subtract),
                            ev=False)
                        e_g = b.op("G", lambda e, _s=sl, _t=tc: e.tensor_tensor(
                            x2p[_s][:, :], x2p[_s][:, :],
                            irep2[:, _t * 1024:(_t + 1) * 1024], OP.mult))
                        b.wait("S", e_g)
                        e_c = b.op("S", lambda e, _s=sl, _k=k: e.copy(
                            x2c[:, _k * 1024:(_k + 1) * 1024], x2p[_s][:, :]))
                        x2ps[sl].free = e_c
                    x2cs.ready = e_c
                    b.ar("T", x2cs)

                    e_mm = None
                    for mat_i, (mname, msrc) in enumerate((("wg", wg_s),
                                                           ("wu", wu_s))):
                        for gi, (g0, gw) in enumerate(FG):
                            base = (gi % 2) * 4
                            for k in range(NKT):
                                wsl = (gi * NKT + k) % 4
                                b.aw("SY", fsls[wsl])
                                if gi == 0 and k == 0 and tc == 0:
                                    for _e in G_DONE[mname]:
                                        b.wait("SY", _e)
                                e_w = b.dma("SY", fsl[wsl][:, 0:gw * 128],
                                            msrc[k * 128:(k + 1) * 128,
                                                 g0 * 128:(g0 + gw) * 128],
                                            sem=f"F{wsl}")
                                fsls[wsl].ready = e_w
                                b.ar("T", fsls[wsl])
                                if k == 0:
                                    for mrel in range(gw):
                                        b.aw("T", pbs[base + 2 * mrel])
                                        b.aw("T", pbs[base + 2 * mrel + 1])
                                for mrel in range(gw):
                                    for tch in range(2):
                                        e_mm = b.op("T", lambda e, _m=mrel, _t=tch,
                                                    _k=k, _w=wsl, _b=base: e.matmul(
                                                        pb[_b + 2 * _m + _t][:, :],
                                                        fsl[_w][:, _m * 128:
                                                             (_m + 1) * 128],
                                                        x2c[:, _k * 1024 + _t * 512:
                                                            _k * 1024 +
                                                            (_t + 1) * 512],
                                                        start=(_k == 0),
                                                        stop=(_k == NKT - 1)),
                                                    ev=(mrel == gw - 1 and
                                                        tch == 1))
                                fsls[wsl].free = e_mm
                            for mrel in range(gw):
                                pbs[base + 2 * mrel].ready = e_mm
                                pbs[base + 2 * mrel + 1].ready = e_mm
                            for mrel in range(gw):
                                m = g0 + mrel
                                if mat_i == 0:
                                    b.ar("S", pbs[base + 2 * mrel])
                                    b.aw("S", hbs[m])
                                    e_ev = None
                                    for tch in range(2):
                                        e_ev = b.op("S", lambda e, _m=m, _r=mrel,
                                                    _t=tch, _b=base: e.activation(
                                                        hbuf[:, _m * 1024 +
                                                             _t * 512:
                                                             _m * 1024 +
                                                             (_t + 1) * 512],
                                                        pb[_b + 2 * _r + _t][:, :],
                                                        AF.Silu), ev=(tch == 1))
                                    pbs[base + 2 * mrel].free = e_ev
                                    pbs[base + 2 * mrel + 1].free = e_ev
                                    hbs[m].ready = e_ev
                                else:
                                    b.ar("S", pbs[base + 2 * mrel])
                                    b.aw("S", hups)
                                    e_ev = None
                                    for tch in range(2):
                                        e_ev = b.op("S", lambda e, _r=mrel, _t=tch,
                                                    _b=base: e.copy(
                                                        hup[:, _t * 512:
                                                            (_t + 1) * 512],
                                                        pb[_b + 2 * _r + _t][:, :]),
                                                    ev=(tch == 1))
                                    pbs[base + 2 * mrel].free = e_ev
                                    pbs[base + 2 * mrel + 1].free = e_ev
                                    hups.ready = e_ev
                                    b.ar("V", hups)
                                    b.ar("V", hbs[m])
                                    e_hm = b.op("V", lambda e, _m=m: e.tensor_tensor(
                                        hbuf[:, _m * 1024:(_m + 1) * 1024],
                                        hbuf[:, _m * 1024:(_m + 1) * 1024],
                                        hup[:, :], OP.mult))
                                    hups.free = e_hm
                                    hbs[m].ready = e_hm
                            if gi % 2 == 1:
                                tick(1)
                        if mat_i == 1:
                            x2cs.free = e_mm
                    tick(10 ** 6)
                    for dq in range(8):
                        base = (dq % 2) * 4
                        e_mm = None
                        for k2 in range(MT):
                            wsl = (dq * MT + k2) % 4
                            b.aw("SY", fsls[wsl])
                            if dq == 0 and k2 == 0 and tc == 0:
                                for _e in G_DONE["wd"]:
                                    b.wait("SY", _e)
                            e_w = b.dma("SY", fsl[wsl][:, 0:256],
                                        wd_s[k2 * 128:(k2 + 1) * 128,
                                             dq * 256:(dq + 1) * 256],
                                        sem=f"F{wsl}")
                            fsls[wsl].ready = e_w
                            b.ar("T", fsls[wsl])
                            if k2 == 0:
                                for mrel in range(2):
                                    b.aw("T", pbs[base + 2 * mrel])
                                    b.aw("T", pbs[base + 2 * mrel + 1])
                                if dq == 0:
                                    for m in range(MT):
                                        b.ar("T", hbs[m])
                            for mrel in range(2):
                                for tch in range(2):
                                    e_mm = b.op("T", lambda e, _m=mrel, _t=tch,
                                                _k=k2, _w=wsl, _b=base: e.matmul(
                                                    pb[_b + 2 * _m + _t][:, :],
                                                    fsl[_w][:, _m * 128:
                                                         (_m + 1) * 128],
                                                    hbuf[:, _k * 1024 + _t * 512:
                                                         _k * 1024 +
                                                         (_t + 1) * 512],
                                                    start=(_k == 0),
                                                    stop=(_k == MT - 1)),
                                                ev=(mrel == 1 and tch == 1))
                            fsls[wsl].free = e_mm
                        if dq == 7:
                            for m in range(MT):
                                hbs[m].free = e_mm
                        for mrel in range(2):
                            pbs[base + 2 * mrel].ready = e_mm
                            pbs[base + 2 * mrel + 1].ready = e_mm
                        for mrel in range(2):
                            m2 = dq * 2 + mrel
                            psl = mrel % 2
                            b.ar("S", pbs[base + 2 * mrel])
                            b.aw("S", yfs[psl])
                            e_y = None
                            for tch in range(2):
                                e_y = b.op("S", lambda e, _r=mrel, _t=tch, _p=psl,
                                           _b=base: e.copy(
                                               yf[_p][:, _t * 512:(_t + 1) * 512],
                                               pb[_b + 2 * _r + _t][:, :]),
                                           ev=(tch == 1))
                            pbs[base + 2 * mrel].free = e_y
                            pbs[base + 2 * mrel + 1].free = e_y
                            yfs[psl].ready = e_y
                            b.aw("SY", xrs[psl])
                            e_xr = b.dma("SY", xr[psl][:, :],
                                         x2_s[m2 * 128:(m2 + 1) * 128,
                                              tc * 1024:(tc + 1) * 1024],
                                         sem=f"R{psl}")
                            xrs[psl].ready = e_xr
                            b.ar("G", yfs[psl])
                            b.ar("G", xrs[psl])
                            b.aw("G", obs[psl])
                            e_o = b.op("G", lambda e, _p=psl: e.tensor_tensor(
                                ob[_p][:, :], yf[_p][:, :], xr[_p][:, :], OP.add))
                            yfs[psl].free = e_o
                            xrs[psl].free = e_o
                            obs[psl].ready = e_o
                            b.ar("SY", obs[psl])
                            e_out = b.dma("SY", outT[m2 * 128:(m2 + 1) * 128,
                                                     tc * 1024:(tc + 1) * 1024],
                                          ob[psl][:, :], sem="Out")
                            obs[psl].free = e_out
                            n_out += 1
                b.wait("SY", ("Out", n_out * 16))
                b.flush("p4")

    return nc


def kernel(**inputs):
    x = np.asarray(inputs["L0_attn_hidden_states"], dtype=np.float32)
    base = {
        "wq": np.ascontiguousarray(np.asarray(inputs["L0_attn_Wq"], dtype=np.float32)),
        "wk": np.ascontiguousarray(np.asarray(inputs["L0_attn_Wk"], dtype=np.float32)),
        "wv": np.ascontiguousarray(np.asarray(inputs["L0_attn_Wv"], dtype=np.float32)),
        "wgt": np.ascontiguousarray(np.asarray(inputs["L0_attn_Wg"], dtype=np.float32)),
        "wo": np.ascontiguousarray(np.asarray(inputs["L0_attn_Wo"], dtype=np.float32)),
        "gn": np.ascontiguousarray(
            np.asarray(inputs["L0_attn_g_norm_weight"], dtype=np.float32).reshape(DV, 1)),
        "lut": np.ascontiguousarray(
            np.asarray(inputs["L0_ffn_lut"], dtype=np.float32).reshape(1, 16)),
        "gidxT": np.ascontiguousarray(np.asarray(inputs["L0_ffn_gate_idx"]).T.astype(np.int32)),
        "uidxT": np.ascontiguousarray(np.asarray(inputs["L0_ffn_up_idx"]).T.astype(np.int32)),
        "didxT": np.ascontiguousarray(np.asarray(inputs["L0_ffn_down_idx"]).T.astype(np.int32)),
    }
    nc = build_program()
    in_maps = []
    for c in range(8):
        m = dict(base)
        m["xT"] = np.ascontiguousarray(x[c].T)
        in_maps.append(m)
    res = run_bass_kernel_spmd(nc, in_maps, core_ids=list(range(8)))
    out = np.stack([np.asarray(res.results[c]["outT"]).T for c in range(8)])
    return out.astype(np.float32)
